# version 29
# baseline (speedup 1.0000x reference)
import numpy as np
from contextlib import ExitStack

import concourse.bass as bass
import concourse.mybir as mybir
from concourse.mybir import AluOpType as A, ActivationFunctionType as ACT
from concourse.bass_utils import run_bass_kernel_spmd
import ml_dtypes

F32 = mybir.dt.float32
BF16 = mybir.dt.bfloat16
FP16 = mybir.dt.float16
I32 = mybir.dt.int32
AX = mybir.AxisListType.X

ROUND_MODE = "ladder"

N_CORES = 8
SEQ = 4096
D = 1024
H = 4096
E3 = 3072
NT = 8
TT = 512
NS = 4
HERON_ITERS = 16


class Eng:

    def __init__(self, e):
        self.e, self.w, self.r = e, set(), set()

    @staticmethod
    def _k(ap):
        t = getattr(ap, "tensor", None)
        return id(t) if t is not None else id(ap)

    def op(self, emitfn, ins=(), outs=()):
        ki = {self._k(a) for a in ins}
        ko = {self._k(a) for a in outs}
        if (ki & self.w) or (ko & (self.w | self.r)):
            self.e.drain()
            self.w.clear()
            self.r.clear()
        res = emitfn()
        self.w |= ko
        self.r |= ki
        return res

    def flush(self):
        self.e.drain()
        self.w.clear()
        self.r.clear()


def _heron(v, nc, W, WH, Y, n):
    v.op(lambda: nc.vector.tensor_scalar(WH[:, :n], W[:, :n], 0.5, None, A.mult),
         ins=[W], outs=[WH])
    v.op(lambda: nc.vector.memset(Y[:, :n], 1.0), outs=[Y])
    for _ in range(HERON_ITERS):
        v.op(lambda: nc.vector.reciprocal(W[:, :n], Y[:, :n]), ins=[Y], outs=[W])
        v.op(lambda: nc.vector.tensor_tensor(W[:, :n], WH[:, :n], W[:, :n], A.mult),
             ins=[WH, W], outs=[W])
        v.op(lambda: nc.vector.scalar_tensor_tensor(Y[:, :n], Y[:, :n], 0.5, W[:, :n], A.mult, A.add),
             ins=[Y, W], outs=[Y])


_LAD_C = [float(128 >> k) / 2.0 for k in range(8)]
_LAD_BIAS = []
_acc = 0.0
for _k in range(8):
    _LAD_BIAS.append(-(float(128 >> _k) + _acc))
    _acc += _LAD_C[_k]


def _round_q(v, nc, src, scale_ap, out_bf, L0, LP, LB, L16):
    v.op(lambda: nc.vector.tensor_scalar(L16, src, scale_ap, 1536.0, A.mult, A.add),
         ins=[src, scale_ap], outs=[L16])
    v.op(lambda: nc.vector.tensor_scalar(out_bf, L16, -1536.0, None, A.add),
         ins=[L16], outs=[out_bf])


def _ladgrp_vec(v, nc, vec, g, srcs, scale_ap, outs, Z, W, SGN, s_ls, s_lv):
    for u in range(2):
        v.op(lambda u=u: nc.vector.tensor_scalar(Z[:, u, :], srcs[u], scale_ap, 128.5, A.mult, A.add),
             ins=[srcs[u], scale_ap], outs=[Z])
        v.op(lambda u=u: nc.vector.tensor_copy(W[:, u, :], Z[:, u, :]), ins=[Z], outs=[W])
        v.flush()
        vec.engine_nop().then_inc(s_lv, 1)
    for b in range(8):
        for u in range(2):
            vec.wait_ge(s_ls, 16 * g + 2 * b + u + 1)
            v.op(lambda u=u, b=b: nc.vector.scalar_tensor_tensor(
                W[:, u, :], SGN[:, u, :], -_LAD_C[b], W[:, u, :], A.mult, A.add),
                ins=[SGN, W], outs=[W])
            v.flush()
            vec.engine_nop().then_inc(s_lv, 1)
    for u in range(2):
        v.op(lambda u=u: nc.vector.scalar_tensor_tensor(
            Z[:, u, :], W[:, u, :], -1.0, Z[:, u, :], A.mult, A.add), ins=[W, Z], outs=[Z])
        v.op(lambda u=u: nc.vector.tensor_scalar(outs[u], Z[:, u, :], -0.5, None, A.add),
             ins=[Z], outs=[outs[u]])


def _ladgrp_scl(s_, nc, sc, g, W, SGN, LADB, s_ls, s_lv):
    for b in range(8):
        for u in range(2):
            if b == 0:
                sc.wait_ge(s_lv, 18 * g + 1 + u)
            else:
                sc.wait_ge(s_lv, 18 * g + 2 * b + 1 + u)
            s_.op(lambda u=u, b=b: nc.scalar.activation(
                SGN[:, u, :], W[:, u, :], ACT.Sign, bias=LADB[:, b:b + 1]),
                ins=[W, LADB], outs=[SGN])
            s_.flush()
            sc.nop().then_inc(s_ls, 1)


def _ladder_vonly(v, nc, src, scale_ap, out_bf, Z, P, B):
    v.op(lambda: nc.vector.tensor_scalar(Z, src, scale_ap, 128.5, A.mult, A.add),
         ins=[src, scale_ap], outs=[Z])
    for k in range(8):
        th = float(128 >> k)
        cur = Z if k == 0 else P
        v.op(lambda cur=cur, th=th: nc.vector.tensor_scalar(B, cur, th, -th, A.is_ge, A.mult),
             ins=[cur], outs=[B])
        v.op(lambda cur=cur: nc.vector.tensor_tensor(P, cur, B, A.add), ins=[cur, B], outs=[P])
    v.op(lambda: nc.vector.tensor_tensor(Z, Z, P, A.subtract), ins=[Z, P], outs=[Z])
    v.op(lambda: nc.vector.tensor_scalar(out_bf, Z, -128.0, None, A.add), ins=[Z], outs=[out_bf])


def _ladgrp4_vec(v, nc, vec, g, srcs, scale_ap, outs, Z, W, SGN, s_ls, s_lv):
    for u in range(4):
        v.op(lambda u=u: nc.vector.tensor_scalar(Z[:, u, :], srcs[u], scale_ap, 128.5, A.mult, A.add),
             ins=[srcs[u], scale_ap], outs=[Z])
        v.op(lambda u=u: nc.vector.tensor_copy(W[:, u, :], Z[:, u, :]), ins=[Z], outs=[W])
        v.flush()
        vec.engine_nop().then_inc(s_lv, 1)
    for b in range(8):
        for u in range(4):
            vec.wait_ge(s_ls, 32 * g + 4 * b + u + 1)
            v.op(lambda u=u, b=b: nc.vector.scalar_tensor_tensor(
                W[:, u, :], SGN[:, u, :], -_LAD_C[b], W[:, u, :], A.mult, A.add),
                ins=[SGN, W], outs=[W])
            v.flush()
            vec.engine_nop().then_inc(s_lv, 1)
    for u in range(4):
        v.op(lambda u=u: nc.vector.scalar_tensor_tensor(
            Z[:, u, :], W[:, u, :], -1.0, Z[:, u, :], A.mult, A.add), ins=[W, Z], outs=[Z])
        v.op(lambda u=u: nc.vector.tensor_scalar(outs[u], Z[:, u, :], -0.5, None, A.add),
             ins=[Z], outs=[outs[u]])


def _ladgrp4_scl(s_, nc, sc, g, W, SGN, LADB, s_ls, s_lv):
    for b in range(8):
        for u in range(4):
            if b == 0:
                sc.wait_ge(s_lv, 36 * g + u + 1)
            else:
                sc.wait_ge(s_lv, 36 * g + 4 * b + u + 1)
            s_.op(lambda u=u, b=b: nc.scalar.activation(
                SGN[:, u, :], W[:, u, :], ACT.Sign, bias=LADB[:, b:b + 1]),
                ins=[W, LADB], outs=[SGN])
            s_.flush()
            sc.nop().then_inc(s_ls, 1)


def build_program():
    nc = bass.Bass("TRN2", target_bir_lowering=False)

    x_h = nc.dram_tensor("x", [SEQ, D], F32, kind="ExternalInput").ap()
    wq_h = nc.dram_tensor("wqkva", [E3, D], F32, kind="ExternalInput").ap()
    gam_h = nc.dram_tensor("gamma", [1, D], F32, kind="ExternalInput").ap()
    g1_h = nc.dram_tensor("ln1g", [1, D], F32, kind="ExternalInput").ap()
    bl1_h = nc.dram_tensor("ln1b", [1, D], F32, kind="ExternalInput").ap()
    w1_h = nc.dram_tensor("w1", [H, D], F32, kind="ExternalInput").ap()
    b1_h = nc.dram_tensor("b1", [1, H], F32, kind="ExternalInput").ap()
    g2_h = nc.dram_tensor("ln2g", [1, H], F32, kind="ExternalInput").ap()
    bl2_h = nc.dram_tensor("ln2b", [1, H], F32, kind="ExternalInput").ap()
    w2_h = nc.dram_tensor("w2", [D, H], F32, kind="ExternalInput").ap()
    b2_h = nc.dram_tensor("b2", [1, D], F32, kind="ExternalInput").ap()
    out_h = nc.dram_tensor("out", [SEQ, D], F32, kind="ExternalOutput").ap()
    w1q_h = nc.dram_tensor("w1q_scr", [H, D], BF16, kind="Internal").ap()
    w2q_h = nc.dram_tensor("w2q_scr", [D, H], BF16, kind="Internal").ap()
    h1_h = nc.dram_tensor("h1_scr", [SEQ, H], BF16, kind="Internal").ap()
    identc_h = nc.inline_tensor(np.eye(128, dtype=ml_dtypes.bfloat16), name="identc").ap()
    onesf_h = nc.inline_tensor(np.ones((1, 128), dtype=np.float32), name="onesf").ap()
    onesb_h = nc.inline_tensor(np.ones((1, 128), dtype=ml_dtypes.bfloat16), name="onesb").ap()

    with ExitStack() as outer:
        SEM = lambda name: outer.enter_context(nc.semaphore(name))
        s_dx0 = SEM("s_dx0"); s_dx1 = SEM("s_dx1")
        s_twq = SEM("s_twq"); s_txr = SEM("s_txr"); s_tat = SEM("s_tat")
        s_so = SEM("s_so")
        s_dga = SEM("s_dga"); s_dw0 = SEM("s_dw0"); s_dw1 = SEM("s_dw1")
        s_gb1r = SEM("s_gb1r"); s_gr = SEM("s_gr"); s_ga = SEM("s_ga")
        s_gw0 = SEM("s_gw0"); s_gw1 = SEM("s_gw1")
        s_gx0 = SEM("s_gx0"); s_gx1 = SEM("s_gx1")
        s_gs0 = SEM("s_gs0"); s_gs1 = SEM("s_gs1")
        s_br1 = SEM("s_br1"); s_bxl = SEM("s_bxl")
        s_tw1w = SEM("s_tw1w"); s_tq1 = SEM("s_tq1")
        s_tw2w = SEM("s_tw2w"); s_tq2 = SEM("s_tq2")
        s_tb = SEM("s_tb"); s_vbc = SEM("s_vbc")
        s_ls = SEM("s_ls"); s_lv = SEM("s_lv")
        s_pg = SEM("s_pg"); s_vsc = SEM("s_vsc"); s_sr = SEM("s_sr"); s_bh2 = SEM("s_bh2"); s_ga2 = SEM("s_ga2"); s_tq1b = SEM("s_tq1b"); s_bxl2 = SEM("s_bxl2"); s_sh1b = SEM("s_sh1b"); s_so2 = SEM("s_so2"); s_so3 = SEM("s_so3"); s_so4 = SEM("s_so4"); s_dx2 = SEM("s_dx2"); s_dx3 = SEM("s_dx3")
        s_vs = SEM("s_vs");  s_vw = SEM("s_vw");  s_ve = SEM("s_ve");  s_va = SEM("s_va")
        s_vc = SEM("s_vc");  s_sx = SEM("s_sx");  s_se = SEM("s_se");  s_mq = SEM("s_mq")
        bar_a = SEM("bar_a")
        s_bx = SEM("s_bx");  s_tw1 = SEM("s_tw1"); s_sh1 = SEM("s_sh1")
        s_su = SEM("s_su");  s_sg = SEM("s_sg");  s_ss = SEM("s_ss")
        s_vq = SEM("s_vq");  s_vd = SEM("s_vd");  s_vr = SEM("s_vr");  s_vp = SEM("s_vp")
        s_vgs = SEM("s_vgs")
        s_m1 = SEM("s_m1");  s_mt = SEM("s_mt")
        bar_b = SEM("bar_b")
        s_bh = SEM("s_bh");  s_tw2 = SEM("s_tw2")
        s_su2 = SEM("s_su2"); s_vq2 = SEM("s_vq2"); s_ve2 = SEM("s_ve2"); s_vp2 = SEM("s_vp2")
        s_m2 = SEM("s_m2");  s_vst2 = SEM("s_vst2")
        bar_c = SEM("bar_c")

        SB = lambda *aa: outer.enter_context(nc.sbuf_tensor(*aa))
        ST_CA = SB("st_ca", [128, 32], F32); ST_CB = SB("st_cb", [128, 32], F32)
        ST_E = SB("st_e", [128, 32], F32);   ST_F = SB("st_f", [128, 32], F32)
        ST_GS = SB("st_gs", [128, 32], F32); ST_SQ = SB("st_sq", [128, 32], F32)
        ST_S1 = SB("st_s1", [128, 32], F32); ST_D1 = SB("st_d1", [128, 32], F32)
        ST_S2 = SB("st_s2", [128, 32], F32); ST_D2 = SB("st_d2", [128, 32], F32)
        SW1 = SB("sw1", [128, 2], F32); SW2 = SB("sw2", [128, 2], F32)
        WRED = SB("wred", [128, 34], F32)
        ONES1 = SB("ones1", [128, 1], F32)
        SCR1 = SB("scr1", [1, 4], F32)
        DQ1R = SB("dq1r", [1, 512], BF16)
        DQ1B = SB("dq1b", [128, 4], BF16)
        STATE = SB("state", [128, 8], F32)
        BNS = SB("bns", [128, 12], F32)
        MV = SB("mv", [128, 8], F32)
        WPK = SB("wpk", [128, 12], F32); WPH = SB("wph", [128, 12], F32); WPY = SB("wpy", [128, 12], F32)
        TCOL = SB("tcol", [128, 8], F32); TC2 = SB("tc2", [128, 8], F32); TC3 = SB("tc3", [128, 16], F32)
        IDENT = SB("ident", [128, 128], BF16)
        ONF = SB("onf", [1, 128], F32)
        ONB = SB("onb", [1, 128], BF16)
        LADB = SB("ladb", [128, 12], F32)

        g_cnt = [0]

        with ExitStack() as stA:
            S = lambda *aa: stA.enter_context(nc.sbuf_tensor(*aa))
            GREP = S("grep", [128, D], F32)
            WQC = S("wqc", [128, 2, D], F32)
            WGB = S("wgb", [128, 2, D], BF16)
            WQT = S("wqt", [128, 8, E3], BF16)
            XT = S("xt", [128, 3, NS, D], F32)
            XR = S("xr", [128, NS, D], BF16)
            XRF = S("xrf", [128, 8, 2 * TT], BF16)
            QF = S("qf", [128, 8, TT], BF16)
            KF = S("kf", [128, 8, TT], BF16)
            AF = S("af", [128, 8, TT], F32)
            KS = S("ks", [128, 8, TT], F32)
            ATB = S("atb", [128, NS, 8, 128], BF16)
            ATM = S("atm", [128, NS, D], BF16)
            PQ = [stA.enter_context(nc.psum_tensor(f"pq{i}", [128, TT], F32)) for i in range(2)]
            blkA = stA.enter_context(nc.Block())


            dx_sems = [s_dx0, s_dx1, s_dx2, s_dx3]
            so_sems = [s_so, s_so2, s_so3, s_so4]

            def ve_wait(m):
                t, j = divmod(m, 24)
                if j < 8:
                    return s_se, 8 * t + j + 1
                if j < 16:
                    return s_ve, 16 * t + (j - 8) + 1
                return s_ve, 16 * t + (j - 7)

            @blkA.sync
            def _(sync):
                sync.dma_start(GREP[0:1, :], gam_h[0:1, :]).then_inc(s_dga, 16)
                sync.dma_start(IDENT[:, :], identc_h[:, :]).then_inc(s_dga, 16)
                sync.dma_start(ONF[0:1, :], onesf_h[0:1, :]).then_inc(s_dga, 16)
                sync.dma_start(ONB[0:1, :], onesb_h[0:1, :]).then_inc(s_dga, 16)
                for e in range(24):
                    if e >= 2:
                        sync.wait_ge(s_vw, e - 1)
                    sync.dma_start(WQC[:, e % 2, :], wq_h[e * 128:(e + 1) * 128, :]).then_inc(
                        s_dw0 if e % 2 == 0 else s_dw1, 16)
                    if e >= 1:
                        sync.wait_ge(s_vw, e)
                        sync.dma_start_transpose(
                            WQT[:, :, (e - 1) * 128:e * 128], WGB[:, (e - 1) % 2, :]
                        ).then_inc(s_twq, 16)
                sync.wait_ge(s_vw, 24)
                sync.dma_start_transpose(WQT[:, :, 23 * 128:24 * 128], WGB[:, 1, :]).then_inc(s_twq, 16)
                for i in range(NT):
                    sync.wait_ge(s_sx, i + 1)
                    if i >= 2:
                        sync.wait_ge(s_mq, 24 * (i - 1))
                    for s in range(NS):
                        sync.dma_start_transpose(
                            XRF[:, :, (i % 2) * TT + s * 128:(i % 2) * TT + (s + 1) * 128], XR[:, s, :]
                        ).then_inc(s_txr, 16)
                    if i >= 1:
                        sync.wait_ge(s_va, i)
                        if i >= 2:
                            sync.wait_ge(s_vc, 4 * (i - 1))
                        for s in range(NS):
                            sync.dma_start_transpose(
                                ATM[:, s, :].rearrange("p (c i) -> p c i", c=8),
                                ATB[:, s, :, :],
                            ).then_inc(s_tat, 16)
                sync.wait_ge(s_va, NT)
                sync.wait_ge(s_vc, 4 * (NT - 1))
                for s in range(NS):
                    sync.dma_start_transpose(
                        ATM[:, s, :].rearrange("p (c i) -> p c i", c=8),
                        ATB[:, s, :, :],
                    ).then_inc(s_tat, 16)
                sync.wait_ge(s_twq, 16 * 24)
                sync.wait_ge(s_txr, 16 * 4 * NT)
                sync.wait_ge(s_tat, 16 * 4 * NT)
                sync.nop().then_inc(bar_a, 1)

            @blkA.gpsimd
            def _(g):
                for j in (0, 1, 2):
                    for s in range(NS):
                        g.dma_start(
                            XT[:, j % 3, s, :],
                            x_h[j * TT + s * 128:j * TT + (s + 1) * 128, :],
                        ).then_inc(s_sr, 16)
                for i in range(1, NT):
                    for s in range(NS):
                        g.wait_ge(s_vc, 4 * (i - 1) + s + 1)
                        g.dma_start(
                            out_h[(i - 1) * TT + s * 128:(i - 1) * TT + (s + 1) * 128, :],
                            XT[:, (i - 1) % 3, s, :],
                        ).then_inc(so_sems[s], 16)
                        if i + 2 < NT:
                            g.wait_ge(so_sems[s], 16 * i)
                            g.dma_start(
                                XT[:, (i + 2) % 3, s, :],
                                x_h[(i + 2) * TT + s * 128:(i + 2) * TT + (s + 1) * 128, :],
                            ).then_inc(dx_sems[s], 16)
                for s in range(NS):
                    g.wait_ge(s_vc, 4 * (NT - 1) + s + 1)
                    g.dma_start(
                        out_h[(NT - 1) * TT + s * 128:(NT - 1) * TT + (s + 1) * 128, :],
                        XT[:, (NT - 1) % 3, s, :],
                    ).then_inc(so_sems[s], 16)
                for s in range(NS):
                    g.wait_ge(so_sems[s], 16 * NT)
                g.nop().then_inc(bar_a, 1)

            @blkA.vector
            def _(vec):
                v = Eng(vec)
                v.op(lambda: nc.vector.memset(ONES1[:, :], 1.0), outs=[ONES1])
                for b in range(8):
                    v.op(lambda b=b: nc.vector.memset(LADB[:, b:b + 1], _LAD_BIAS[b] + 128.0), outs=[LADB])
                v.op(lambda: nc.vector.memset(LADB[:, 8:9], 128.5), outs=[LADB])
                v.op(lambda: nc.vector.memset(LADB[:, 9:10], 0.5), outs=[LADB])
                v.op(lambda: nc.vector.memset(STATE[:, :], 0.0), outs=[STATE])
                for c in range(2):
                    vec.wait_ge(s_tb, c + 1)
                    v.op(lambda c=c: nc.vector.tensor_copy(GREP[:, 512 * c:512 * (c + 1)], PQ[0][:, :]),
                         ins=[PQ[0]], outs=[GREP])
                    v.flush()
                    vec.engine_nop().then_inc(s_vbc, 1)
                for e in range(24):
                    vec.wait_ge(s_dw0 if e % 2 == 0 else s_dw1, 16 * (e // 2 + 1))
                    if e >= 2:
                        vec.wait_ge(s_twq, 16 * e)
                    v.op(lambda e=e: nc.vector.tensor_tensor(
                        WGB[:, e % 2, :], WQC[:, e % 2, :], GREP[:, :], A.mult),
                        ins=[WQC, GREP], outs=[WGB])
                    v.flush()
                    vec.engine_nop().then_inc(s_vw, 1)
                def emit_stats(j):
                    p = j % 3
                    c0 = 4 * j
                    if j <= 2:
                        vec.wait_ge(s_sr, 16 * 12)
                    for s in range(NS):
                        if j >= 3:
                            vec.wait_ge(dx_sems[s], 16 * (j - 2))
                        v.op(lambda s=s, p=p: nc.vector.bn_stats(BNS[:, 0:6], XT[:, p, s, 0:512]),
                             ins=[XT], outs=[BNS])
                        v.op(lambda s=s, p=p: nc.vector.bn_stats(BNS[:, 6:12], XT[:, p, s, 512:1024]),
                             ins=[XT], outs=[BNS])
                        v.op(lambda s=s, p=p: nc.vector.bn_aggr(MV[:, 2 * s:2 * s + 2], BNS[:, :]),
                             ins=[BNS], outs=[MV])
                    v.op(lambda: nc.vector.tensor_copy(TCOL[:, 0:4], MV[:, 0:8:2]), ins=[MV], outs=[TCOL])
                    v.op(lambda: nc.vector.tensor_copy(TCOL[:, 4:8], MV[:, 1:8:2]), ins=[MV], outs=[TCOL])
                    v.op(lambda: nc.vector.tensor_scalar(WPK[:, 0:4], TCOL[:, 4:8], 1e-5, None, A.add),
                         ins=[TCOL], outs=[WPK])
                    v.op(lambda: nc.vector.reciprocal(TC2[:, 0:4], WPK[:, 0:4]), ins=[WPK], outs=[TC2])
                    v.op(lambda: nc.vector.tensor_tensor(TC2[:, 4:8], TCOL[:, 4:8], TC2[:, 0:4], A.mult),
                         ins=[TCOL, TC2], outs=[TC2])
                    v.op(lambda: nc.vector.tensor_scalar(WPK[:, 4:8], TC2[:, 4:8], 1e-5, None, A.add),
                         ins=[TC2], outs=[WPK])
                    v.op(lambda: nc.vector.tensor_scalar(WPK[:, 8:12], TC2[:, 4:8], 1024.0, None, A.mult),
                         ins=[TC2], outs=[WPK])
                    _heron(v, nc, WPK, WPH, WPY, 12)
                    v.op(lambda: nc.vector.reciprocal(TC3[:, 0:4], WPY[:, 0:4]), ins=[WPY], outs=[TC3])
                    v.op(lambda: nc.vector.reciprocal(TC3[:, 4:8], WPY[:, 4:8]), ins=[WPY], outs=[TC3])
                    v.op(lambda: nc.vector.tensor_scalar(WPY[:, 8:12], WPY[:, 8:12], 1e-12, None, A.max),
                         ins=[WPY], outs=[WPY])
                    v.op(lambda: nc.vector.reciprocal(TC2[:, 0:4], WPY[:, 8:12]), ins=[WPY], outs=[TC2])
                    v.op(lambda: nc.vector.tensor_scalar(TC2[:, 0:4], TC2[:, 0:4], 32.0, None, A.mult),
                         ins=[TC2], outs=[TC2])
                    v.op(lambda c0=c0: nc.vector.tensor_tensor(
                        ST_CA[:, c0:c0 + 4], TC3[:, 0:4], TC2[:, 0:4], A.mult),
                        ins=[TC3, TC2], outs=[ST_CA])
                    v.op(lambda c0=c0: nc.vector.scalar_tensor_tensor(
                        ST_CB[:, c0:c0 + 4], TCOL[:, 0:4], -1.0, ST_CA[:, c0:c0 + 4], A.mult, A.mult),
                        ins=[TCOL, ST_CA], outs=[ST_CB])
                    v.op(lambda c0=c0: nc.vector.tensor_tensor(
                        ST_E[:, c0:c0 + 4], TC3[:, 0:4], TC3[:, 4:8], A.mult),
                        ins=[TC3], outs=[ST_E])
                    v.op(lambda c0=c0: nc.vector.scalar_tensor_tensor(
                        ST_F[:, c0:c0 + 4], TCOL[:, 0:4], -1.0, ST_E[:, c0:c0 + 4], A.mult, A.mult),
                        ins=[TCOL, ST_E], outs=[ST_F])
                    v.flush()
                    vec.engine_nop().then_inc(s_vs, 1)

                emit_stats(0)
                for i in range(NT):
                    p = i % 3
                    c0 = 4 * i
                    if i + 1 < NT:
                        emit_stats(i + 1)
                    if i >= 1:
                        vec.wait_ge(s_tat, 64 * i)
                        for s in range(NS):
                            v.op(lambda s=s, i=i: nc.vector.tensor_tensor(
                                XT[:, (i - 1) % 3, s, :], XT[:, (i - 1) % 3, s, :],
                                ATM[:, s, :], A.add), ins=[XT, ATM], outs=[XT])
                            v.flush()
                            vec.engine_nop().then_inc(s_vc, 1)
                    for dc in range(8):
                        m = 24 * i + 8 + dc
                        vec.wait_ge(s_mq, m + 1)
                        v.op(lambda dc=dc, m=m: nc.vector.tensor_copy(KF[:, dc, :], PQ[m % 2][:, :]),
                             ins=[PQ[m % 2]], outs=[KF])
                        v.flush()
                        vec.engine_nop().then_inc(s_ve, 1)
                        vec.wait_ge(s_se, 8 * i + dc + 1)
                        v.op(lambda dc=dc: nc.vector.tensor_tensor_scan(
                            KS[:, dc, :], AF[:, dc, :], KF[:, dc, :], STATE[:, dc:dc + 1], A.mult, A.add),
                            ins=[AF, KF, STATE], outs=[KS])
                        v.op(lambda dc=dc: nc.vector.tensor_copy(STATE[:, dc:dc + 1], KS[:, dc, TT - 1:TT]),
                             ins=[KS], outs=[STATE])
                    for dc in range(8):
                        m = 24 * i + 16 + dc
                        vec.wait_ge(s_mq, m + 1)
                        v.op(lambda dc=dc, m=m: nc.vector.tensor_copy(QF[:, dc, :], PQ[m % 2][:, :]),
                             ins=[PQ[m % 2]], outs=[QF])
                        v.flush()
                        vec.engine_nop().then_inc(s_ve, 1)
                    v.op(lambda: nc.vector.tensor_tensor(
                        ATB[:, :, :, :].rearrange("p s c j -> p c s j"),
                        QF[:, :, :].rearrange("p c (s j) -> p c s j", s=NS),
                        KS[:, :, :].rearrange("p c (s j) -> p c s j", s=NS), A.mult),
                         ins=[QF, KS], outs=[ATB])
                    v.flush()
                    vec.engine_nop().then_inc(s_va, 1)
                vec.wait_ge(s_tat, 64 * NT)
                for s in range(NS):
                    v.op(lambda s=s: nc.vector.tensor_tensor(
                        XT[:, (NT - 1) % 3, s, :], XT[:, (NT - 1) % 3, s, :],
                        ATM[:, s, :], A.add), ins=[XT, ATM], outs=[XT])
                    v.flush()
                    vec.engine_nop().then_inc(s_vc, 1)
                vec.nop().then_inc(bar_a, 1)

            @blkA.scalar
            def _(sc):
                s_ = Eng(sc)
                for i in range(NT):
                    p = i % 3
                    c0 = 4 * i
                    sc.wait_ge(s_vs, i + 1)
                    if i >= 1:
                        sc.wait_ge(s_txr, 16 * 4 * i)
                    for s in range(NS):
                        s_.op(lambda s=s, c0=c0, p=p: nc.scalar.activation(
                            XR[:, s, :], XT[:, p, s, :], ACT.Identity,
                            bias=ST_CB[:, c0 + s:c0 + s + 1], scale=ST_CA[:, c0 + s:c0 + s + 1]),
                            ins=[XT, ST_CA, ST_CB], outs=[XR])
                    s_.flush()
                    sc.nop().then_inc(s_sx, 1)
                    if i >= 1:
                        sc.wait_ge(s_va, i)
                    for j in range(8):
                        m = 24 * i + j
                        sc.wait_ge(s_mq, m + 1)
                        s_.op(lambda j=j, m=m: nc.scalar.activation(
                            AF[:, j, :], PQ[m % 2][:, :], ACT.Sigmoid),
                            ins=[PQ[m % 2]], outs=[AF])
                        s_.flush()
                        sc.nop().then_inc(s_se, 1)
                sc.nop().then_inc(bar_a, 1)

            @blkA.tensor
            def _(t):
                t.wait_ge(s_dga, 64)
                for c in range(2):
                    if c >= 1:
                        t.wait_ge(s_vbc, c)
                    t.matmul(PQ[0][:, :], ONF[0:1, :], GREP[0:1, 512 * c:512 * (c + 1)],
                             start=True, stop=True).then_inc(s_tb, 1)
                t.wait_ge(s_vbc, 2)
                if True:
                    t.wait_ge(s_twq, 16 * 24)
                elist = list(range(16, 24)) + list(range(8, 16)) + list(range(0, 8))
                for i in range(NT):
                    t.wait_ge(s_txr, 16 * 4 * (i + 1))
                    for j, e in enumerate(elist):
                        m = 24 * i + j
                        if m >= 2:
                            sem, val = ve_wait(m - 2)
                            t.wait_ge(sem, val)
                        ins = None
                        for dc in range(8):
                            ins = t.matmul(
                                PQ[m % 2][:, :],
                                WQT[:, dc, e * 128:(e + 1) * 128],
                                XRF[:, dc, (i % 2) * TT:(i % 2 + 1) * TT],
                                start=(dc == 0), stop=(dc == 7))
                        ins.then_inc(s_mq, 1)
                t.nop().then_inc(bar_a, 1)

        with ExitStack() as stB1:
            S = lambda *aa: stB1.enter_context(nc.sbuf_tensor(*aa))
            G1REP = S("g1rep", [128, D], F32)
            BL1REP = S("bl1rep", [128, D], F32)
            B1ROW = S("b1row", [1, H], BF16)
            W1C = S("w1c", [128, 2, D], F32)
            W1QB = S("w1qb", [128, 2, D], BF16)
            W1T = S("w1t", [128, 8, H], BF16)
            XBT = S("xbt", [128, NS, D], F32)
            U1 = S("u1", [128, 3, D], F32)
            QX1 = S("qx1", [128, 2, D], BF16)
            XF1 = S("xf1", [128, 8, 256], BF16)
            H1T = S("h1t", [128, NS, H], BF16)
            SQD = S("sqd", [128, H], BF16)
            GT4 = S("gt4", [128, 32, 4], F32)
            R1 = S("r1", [128, 2, D], F32)
            B1L = S("b1l", [128, D], F32)
            L16 = S("l16", [128, D], FP16) if ROUND_MODE == "fp16" else None
            PP1 = [stB1.enter_context(nc.psum_tensor(f"pp1{i}", [128, 1024], F32)) for i in range(2)]
            PB1DQ = stB1.enter_context(nc.psum_tensor("pb1dq", [128, 512], BF16))
            P1X = stB1.enter_context(nc.psum_tensor("p1x", [128, 2], F32))
            blkB1 = stB1.enter_context(nc.Block())

            gw_l = lambda ch: 16 * (ch // 2 + 1)
            gs_st = lambda ch: 16 * (ch // 2 + 1)
            bx_x = lambda i: 16 * (i + 1)

            @blkB1.sync
            def _(sync):
                sync.wait_ge(bar_a, 5)
                sync.dma_start(G1REP[0:1, :], g1_h[0:1, :]).then_inc(s_br1, 16)
                sync.dma_start(BL1REP[0:1, :], bl1_h[0:1, :]).then_inc(s_br1, 16)
                sync.wait_ge(s_gs0, 16 * 16)
                sync.wait_ge(s_gs1, 16 * 16)
                for dc in range(8):
                    sync.dma_start_transpose(W1T[:, dc, :], w1q_h[:, dc * 128:(dc + 1) * 128]).then_inc(s_tw1w, 16)

                def st_sub(m):
                    sync.wait_ge(s_ss, m + 1)
                    sync.dma_start(
                        h1_h[m * 128:(m + 1) * 128, :], H1T[:, m % 4, :]
                    ).then_inc(s_sh1 if m % 2 == 0 else s_sh1b, 16)

                def xf_sub(m):
                    sync.wait_ge(s_vq, m + 1)
                    if m >= 2:
                        sync.wait_ge(s_m1, 4 * (m - 1))
                    sync.dma_start_transpose(
                        XF1[:, :, (m % 2) * 128:(m % 2 + 1) * 128], QX1[:, m % 2, :]
                    ).then_inc(s_tq1 if m % 2 == 0 else s_tq1b, 16)

                for n in range(NT * NS):
                    if n >= 4:
                        sync.wait_ge(s_su, n - 3)
                    sync.dma_start(
                        XBT[:, n % 4, :],
                        x_h[(n // 4) * TT + (n % 4) * 128:(n // 4) * TT + (n % 4 + 1) * 128, :],
                    ).then_inc(s_bxl if n % 2 == 0 else s_bxl2, 16)
                    if n >= 1:
                        xf_sub(n - 1)
                    if n >= 3:
                        st_sub(n - 3)
                xf_sub(NT * NS - 1)
                for m in range(NT * NS - 3, NT * NS):
                    st_sub(m)
                sync.wait_ge(s_sh1, 16 * (NT * NS // 2))
                sync.wait_ge(s_sh1b, 16 * (NT * NS // 2))
                sync.nop().then_inc(bar_b, 1)

            @blkB1.gpsimd
            def _(g):
                gw = Eng(g)
                g.wait_ge(bar_a, 5)
                g.dma_start(B1ROW[0:1, :], b1_h[0:1, :]).then_inc(s_gb1r, 16)
                for ch in range(32):
                    if ch >= 2:
                        g.wait_ge(s_vp, ch - 1)
                    g.dma_start(W1C[:, ch % 2, :], w1_h[ch * 128:(ch + 1) * 128, :]).then_inc(
                        s_gw0 if ch % 2 == 0 else s_gw1, 16)
                g.wait_ge(s_vp, 34)
                for ch in range(32):
                    if ch >= 2:
                        g.wait_ge(s_vp, 34 + ch - 1)
                    g.dma_start(W1C[:, ch % 2, :], w1_h[ch * 128:(ch + 1) * 128, :]).then_inc(
                        s_gx0 if ch % 2 == 0 else s_gx1, 16)
                    if ch >= 1:
                        g.wait_ge(s_vp, 34 + ch)
                        g.dma_start(w1q_h[(ch - 1) * 128:ch * 128, :], W1QB[:, (ch - 1) % 2, :]).then_inc(
                            s_gs0 if (ch - 1) % 2 == 0 else s_gs1, 16)
                g.wait_ge(s_vp, 34 + 32)
                g.dma_start(w1q_h[31 * 128:32 * 128, :], W1QB[:, 1, :]).then_inc(s_gs1, 16)
                g.nop().then_inc(bar_b, 1)

            @blkB1.vector
            def _(vec):
                v = Eng(vec)
                vec.wait_ge(bar_a, 5)
                for k, (dst, base) in enumerate([(G1REP, 0), (G1REP, 512), (BL1REP, 0), (BL1REP, 512)]):
                    vec.wait_ge(s_tb, 3 + k)
                    v.op(lambda dst=dst, base=base: nc.vector.tensor_copy(
                        dst[:, base:base + 512], PP1[0][:, 0:512]), ins=[PP1[0]], outs=[dst])
                    v.flush()
                    vec.engine_nop().then_inc(s_vbc, 1)
                for ch in range(32):
                    vec.wait_ge(s_gw0 if ch % 2 == 0 else s_gw1, gw_l(ch))
                    v.op(lambda ch=ch: nc.vector.reduce_sum(
                        WRED[:, ch:ch + 1], W1C[:, ch % 2, :], axis=AX, apply_absolute_value=True),
                        ins=[W1C], outs=[WRED])
                    v.flush()
                    vec.engine_nop().then_inc(s_vp, 1)
                v.op(lambda: nc.vector.reduce_sum(WRED[:, 32:33], WRED[:, 0:32], axis=AX),
                     ins=[WRED], outs=[WRED])
                v.flush()
                vec.engine_nop().then_inc(s_vp, 1)
                vec.wait_ge(s_mt, 1)
                v.op(lambda: nc.vector.tensor_copy(SCR1[0:1, 2:3], P1X[0:1, 0:1]), ins=[P1X], outs=[SCR1])
                v.op(lambda: nc.vector.tensor_scalar(SCR1[0:1, 2:3], SCR1[0:1, 2:3],
                                                     1.0 / (4096.0 * 1024.0), 1e-5, A.mult, A.max),
                     ins=[SCR1], outs=[SCR1])
                v.op(lambda: nc.vector.tensor_copy(SCR1[0:1, 1:2], SCR1[0:1, 2:3]), ins=[SCR1], outs=[SCR1])
                v.op(lambda: nc.vector.reciprocal(SCR1[0:1, 0:1], SCR1[0:1, 2:3]), ins=[SCR1], outs=[SCR1])
                v.flush()
                vec.engine_nop().then_inc(s_vp, 1)
                vec.wait_ge(s_tb, 7)
                v.op(lambda: nc.vector.tensor_copy(SW1[:, :], P1X[:, 0:2]), ins=[P1X], outs=[SW1])
                v.flush()
                vec.engine_nop().then_inc(s_vbc, 1)
                for ch in range(32):
                    vec.wait_ge(s_gx0 if ch % 2 == 0 else s_gx1, gw_l(ch))
                    if ch >= 2:
                        vec.wait_ge(s_gs0 if ch % 2 == 0 else s_gs1, gs_st(ch - 2))
                    v.op(lambda ch=ch: nc.vector.tensor_scalar(
                        R1[:, 0, :], W1C[:, ch % 2, :], SW1[:, 0:1], 0.5, A.mult, A.is_ge),
                        ins=[W1C, SW1], outs=[R1])
                    v.op(lambda ch=ch: nc.vector.tensor_scalar(
                        R1[:, 1, :], W1C[:, ch % 2, :], SW1[:, 0:1], -0.5, A.mult, A.is_lt),
                        ins=[W1C, SW1], outs=[R1])
                    v.op(lambda ch=ch: nc.vector.tensor_tensor(
                        W1QB[:, ch % 2, :], R1[:, 0, :], R1[:, 1, :], A.subtract),
                        ins=[R1], outs=[W1QB])
                    v.flush()
                    vec.engine_nop().then_inc(s_vp, 1)

                def v_pre(n):
                    sl = n % 4
                    vec.wait_ge(s_su, n + 1)
                    if n >= 4:
                        vec.wait_ge(s_mt, n - 2)
                    v.op(lambda n=n: nc.vector.tensor_tensor(
                        U1[:, n % 3, :], U1[:, n % 3, :], G1REP[:, :], A.mult), ins=[U1, G1REP], outs=[U1])
                    v.op(lambda n=n: nc.vector.tensor_tensor(
                        U1[:, n % 3, :], U1[:, n % 3, :], BL1REP[:, :], A.add), ins=[U1, BL1REP], outs=[U1])
                    v.op(lambda sl=sl, n=n: nc.vector.reduce_max(
                        TCOL[:, sl:sl + 1], U1[:, n % 3, :], axis=AX, apply_absolute_value=True),
                        ins=[U1], outs=[TCOL])
                    v.op(lambda sl=sl: nc.vector.tensor_scalar(
                        TC2[:, sl:sl + 1], TCOL[:, sl:sl + 1], 1e-5, 1.0 / 127.0, A.max, A.mult),
                        ins=[TCOL], outs=[TC2])
                    v.op(lambda sl=sl, n=n: nc.vector.reciprocal(
                        ST_S1[:, n:n + 1], TC2[:, sl:sl + 1]), ins=[TC2], outs=[ST_S1])
                    v.op(lambda sl=sl, n=n: nc.vector.tensor_scalar(
                        ST_D1[:, n:n + 1], TC2[:, sl:sl + 1], SW1[:, 1:2], None, A.mult),
                        ins=[TC2, SW1], outs=[ST_D1])
                    v.op(lambda sl=sl, n=n: nc.vector.tensor_scalar(
                        DQ1B[:, sl:sl + 1], ST_S1[:, n:n + 1], SW1[:, 0:1], None, A.mult),
                        ins=[ST_S1, SW1], outs=[DQ1B])
                    v.flush()
                    vec.engine_nop().then_inc(s_vd, 1)

                def v_dq(m):
                    sl = m % 4
                    vec.wait_ge(s_mt, m + 2)
                    if m >= 4:
                        vec.wait_ge(s_m1, 4 * (m - 3))
                    v.op(lambda sl=sl: nc.vector.tensor_copy(
                        DQ1R[0:1, sl * 128:(sl + 1) * 128], PB1DQ[0:1, sl * 128:(sl + 1) * 128]),
                        ins=[PB1DQ], outs=[DQ1R])
                    v.flush()
                    vec.engine_nop().then_inc(s_vr, 1)

                def v_peel(m):
                    if m >= 2:
                        vec.wait_ge(s_tq1 if m % 2 == 0 else s_tq1b,
                                    16 * (m // 2))
                    vec.wait_ge(s_ls, m + 1)
                    if ROUND_MODE == "fp16":
                        _round_q(v, nc, U1[:, m % 3, :], ST_S1[:, m:m + 1], QX1[:, m % 2, :],
                                 None, None, None, L16[:, :])
                    else:
                        Rh = R1[:, m % 2, :]
                        for k in range(8):
                            th = float(128 >> k)
                            v.op(lambda th=th, Rh=Rh: nc.vector.tensor_scalar(
                                B1L[:, :], Rh, th, -th, A.is_ge, A.mult), ins=[Rh], outs=[B1L])
                            v.op(lambda Rh=Rh: nc.vector.tensor_tensor(
                                Rh, Rh, B1L[:, :], A.add), ins=[Rh, B1L], outs=[Rh])
                        v.op(lambda Rh=Rh: nc.vector.tensor_scalar(
                            Rh, Rh, -0.5, None, A.add), ins=[Rh], outs=[Rh])
                        v.op(lambda m=m, Rh=Rh: nc.vector.scalar_tensor_tensor(
                            QX1[:, m % 2, :], U1[:, m % 3, :], ST_S1[:, m:m + 1], Rh,
                            A.mult, A.subtract), ins=[U1, ST_S1, R1], outs=[QX1])
                    v.flush()
                    vec.engine_nop().then_inc(s_vq, 1)

                for n in range(NT * NS):
                    v_pre(n)
                    if n >= 1:
                        v_dq(n - 1)
                        v_peel(n - 1)
                v_dq(NT * NS - 1)
                v_peel(NT * NS - 1)
                vec.wait_ge(s_sg, 4 * NT * NS)
                for col in range(32):
                    v.op(lambda col=col: nc.vector.reduce_sum(
                        ST_GS[:, col:col + 1], GT4[:, col, :], axis=AX),
                        ins=[GT4], outs=[ST_GS])
                v.flush()
                vec.engine_nop().then_inc(s_vgs, 1)
                vec.nop().then_inc(bar_b, 1)

            @blkB1.scalar
            def _(sc):
                s_ = Eng(sc)
                sc.wait_ge(bar_a, 5)

                def s_u1(n):
                    sl = n % 4
                    sc.wait_ge(s_bxl if n % 2 == 0 else s_bxl2, 16 * (n // 2 + 1))
                    if n >= 3:
                        sc.wait_ge(s_vq, n - 2)
                    s_.op(lambda sl=sl, n=n: nc.scalar.activation(
                        U1[:, n % 3, :], XBT[:, sl, :], ACT.Identity,
                        bias=ST_F[:, n:n + 1], scale=ST_E[:, n:n + 1]),
                        ins=[XBT, ST_E, ST_F], outs=[U1])
                    s_.flush()
                    sc.nop().then_inc(s_su, 1)

                def s_z0(m):
                    sc.wait_ge(s_vd, m + 1)
                    s_.op(lambda m=m: nc.scalar.activation(
                        R1[:, m % 2, :], U1[:, m % 3, :], ACT.Identity,
                        bias=LADB[:, 8:9], scale=ST_S1[:, m:m + 1]),
                        ins=[U1, ST_S1, LADB], outs=[R1])
                    s_.flush()
                    sc.nop().then_inc(s_ls, 1)

                def s_gel(m):
                    sl = m % 4
                    if m >= 2:
                        sc.wait_ge(s_sh1 if m % 2 == 0 else s_sh1b,
                                   16 * (m // 2))
                    for gix in range(4):
                        gl = 4 * m + gix
                        sc.wait_ge(s_m1, gl + 1)
                        s_.op(lambda sl=sl, gix=gix, gl=gl, m=m: nc.scalar.activation(
                            H1T[:, sl, 1024 * gix:1024 * (gix + 1)], PP1[gl % 2][:, :], ACT.Gelu,
                            scale=ST_D1[:, m:m + 1],
                            accum_out=GT4[:, m, gix:gix + 1]),
                            ins=[PP1[gl % 2], ST_D1], outs=[H1T, GT4])
                        s_.flush()
                        sc.nop().then_inc(s_sg, 1)
                    s_.op(lambda sl=sl, m=m: nc.scalar.activation(
                        SQD[:, :], H1T[:, sl, :], ACT.Square, accum_out=ST_SQ[:, m:m + 1]),
                        ins=[H1T], outs=[SQD, ST_SQ])
                    s_.flush()
                    sc.nop().then_inc(s_ss, 1)

                for n in range(NT * NS):
                    s_u1(n)
                    if n >= 1:
                        s_z0(n - 1)
                    if n >= 2:
                        s_gel(n - 2)
                s_z0(NT * NS - 1)
                s_gel(NT * NS - 2)
                s_gel(NT * NS - 1)
                sc.nop().then_inc(bar_b, 1)

            @blkB1.tensor
            def _(t):
                t.wait_ge(bar_a, 5)
                t.wait_ge(s_gb1r, 16)
                t.wait_ge(s_br1, 32)
                for k, (dst, base) in enumerate([(G1REP, 0), (G1REP, 512), (BL1REP, 0), (BL1REP, 512)]):
                    t.wait_ge(s_vbc, 2 + k)
                    t.matmul(PP1[0][:, 0:512], ONF[0:1, :], dst[0:1, base:base + 512],
                             start=True, stop=True).then_inc(s_tb, 1)
                t.wait_ge(s_vbc, 6)
                t.wait_ge(s_vp, 33)
                t.matmul(P1X[0:1, 0:1], WRED[:, 32:33], ONES1[:, :], start=True, stop=True).then_inc(s_mt, 1)
                t.wait_ge(s_vp, 34)
                t.matmul(P1X[:, 0:2], ONF[0:1, :], SCR1[0:1, 0:2], start=True, stop=True).then_inc(s_tb, 1)

                def m1_sub(m):
                    for gix in range(4):
                        gl = 4 * m + gix
                        if gix == 0:
                            if m == 0:
                                t.wait_ge(s_tw1w, 16 * 8)
                            t.wait_ge(s_tq1 if m % 2 == 0 else s_tq1b, 16 * (m // 2 + 1))
                            t.wait_ge(s_vr, m + 1)
                        if gl >= 2:
                            t.wait_ge(s_sg, gl - 1)
                        pp = PP1[gl % 2]
                        ins = None
                        for half in range(2):
                            hi = 1024 * gix + 512 * half
                            for dc in range(8):
                                t.matmul(pp[:, 512 * half:512 * (half + 1)],
                                         XF1[:, dc, (m % 2) * 128:(m % 2 + 1) * 128],
                                         W1T[:, dc, hi:hi + 512],
                                         start=(dc == 0), stop=False)
                            ins = t.matmul(pp[:, 512 * half:512 * (half + 1)],
                                           DQ1R[0:1, (m % 4) * 128:(m % 4 + 1) * 128],
                                           B1ROW[0:1, hi:hi + 512],
                                           start=False, stop=True)
                        ins.then_inc(s_m1, 1)

                for n in range(NT * NS):
                    t.wait_ge(s_vd, n + 1)
                    if n >= 4:
                        t.wait_ge(s_vr, n - 3)
                    t.transpose(PB1DQ[0:1, (n % 4) * 128:(n % 4 + 1) * 128],
                                DQ1B[:, n % 4:n % 4 + 1], IDENT[:, :]).then_inc(s_mt, 1)
                    if n >= 1:
                        m1_sub(n - 1)
                m1_sub(NT * NS - 1)
                t.nop().then_inc(bar_b, 1)

        with ExitStack() as stW2o:
            W2T = stW2o.enter_context(nc.sbuf_tensor("w2t", [128, 32, D], BF16))
            with ExitStack() as stW2:
                S = lambda *aa: stW2.enter_context(nc.sbuf_tensor(*aa))
                W2C = S("w2c", [128, 2, H], F32)
                W2QB = S("w2qb", [128, 2, H], BF16)
                WL1b = S("wl1b", [128, H], F32)
                WL2b = S("wl2b", [128, H], F32)
                P2X = stW2.enter_context(nc.psum_tensor("p2x", [128, 2], F32))
                blkW2 = stW2.enter_context(nc.Block())

                g2w_l = lambda ch: 16 * (16 + ch // 2 + 1)
                g2s_st = lambda ch: 16 * (16 + ch // 2 + 1)

                @blkW2.sync
                def _(sync):
                    sync.wait_ge(bar_b, 5)
                    sync.wait_ge(s_gs0, 16 * 20)
                    sync.wait_ge(s_gs1, 16 * 20)
                    for hc in range(32):
                        sync.dma_start_transpose(
                            W2T[:, hc, :], w2q_h[:, hc * 128:(hc + 1) * 128]).then_inc(s_tw2w, 16)
                    sync.nop().then_inc(bar_c, 1)

                @blkW2.gpsimd
                def _(g):
                    gw = Eng(g)
                    g.wait_ge(bar_b, 5)
                    for ch in range(8):
                        if ch >= 2:
                            g.wait_ge(s_vp2, ch - 1)
                        g.dma_start(W2C[:, ch % 2, :], w2_h[ch * 128:(ch + 1) * 128, :]).then_inc(
                            s_gw0 if ch % 2 == 0 else s_gw1, 16)
                    g.wait_ge(s_vp2, 10)
                    for ch in range(8):
                        if ch >= 2:
                            g.wait_ge(s_vp2, 10 + ch - 1)
                        g.dma_start(W2C[:, ch % 2, :], w2_h[ch * 128:(ch + 1) * 128, :]).then_inc(
                            s_gx0 if ch % 2 == 0 else s_gx1, 16)
                        if ch >= 1:
                            g.wait_ge(s_vp2, 10 + ch)
                            g.dma_start(w2q_h[(ch - 1) * 128:ch * 128, :], W2QB[:, (ch - 1) % 2, :]).then_inc(
                                s_gs0 if (ch - 1) % 2 == 0 else s_gs1, 16)
                    g.wait_ge(s_vp2, 10 + 8)
                    g.dma_start(w2q_h[7 * 128:8 * 128, :], W2QB[:, 1, :]).then_inc(s_gs1, 16)
                    g.nop().then_inc(bar_c, 1)

                @blkW2.vector
                def _(vec):
                    v = Eng(vec)
                    vec.wait_ge(bar_b, 5)
                    for ch in range(8):
                        vec.wait_ge(s_gw0 if ch % 2 == 0 else s_gw1, g2w_l(ch))
                        v.op(lambda ch=ch: nc.vector.reduce_sum(
                            WRED[:, ch:ch + 1], W2C[:, ch % 2, :], axis=AX, apply_absolute_value=True),
                            ins=[W2C], outs=[WRED])
                        v.flush()
                        vec.engine_nop().then_inc(s_vp2, 1)
                    v.op(lambda: nc.vector.reduce_sum(WRED[:, 32:33], WRED[:, 0:8], axis=AX),
                         ins=[WRED], outs=[WRED])
                    v.flush()
                    vec.engine_nop().then_inc(s_vp2, 1)
                    vec.wait_ge(s_mt, NT * NS + 2)
                    v.op(lambda: nc.vector.tensor_copy(SCR1[0:1, 2:3], P2X[0:1, 0:1]), ins=[P2X], outs=[SCR1])
                    v.op(lambda: nc.vector.tensor_scalar(SCR1[0:1, 2:3], SCR1[0:1, 2:3],
                                                         1.0 / (4096.0 * 1024.0), 1e-5, A.mult, A.max),
                         ins=[SCR1], outs=[SCR1])
                    v.op(lambda: nc.vector.tensor_copy(SCR1[0:1, 1:2], SCR1[0:1, 2:3]), ins=[SCR1], outs=[SCR1])
                    v.op(lambda: nc.vector.reciprocal(SCR1[0:1, 0:1], SCR1[0:1, 2:3]), ins=[SCR1], outs=[SCR1])
                    v.flush()
                    vec.engine_nop().then_inc(s_vp2, 1)
                    vec.wait_ge(s_tb, 8)
                    v.op(lambda: nc.vector.tensor_copy(SW2[:, :], P2X[:, 0:2]), ins=[P2X], outs=[SW2])
                    v.flush()
                    vec.engine_nop().then_inc(s_vbc, 1)
                    for ch in range(8):
                        vec.wait_ge(s_gx0 if ch % 2 == 0 else s_gx1, g2w_l(ch))
                        if ch >= 2:
                            vec.wait_ge(s_gs0 if ch % 2 == 0 else s_gs1, g2s_st(ch - 2))
                        v.op(lambda ch=ch: nc.vector.tensor_scalar(
                            WL1b[:, :], W2C[:, ch % 2, :], SW2[:, 0:1], 0.5, A.mult, A.is_ge),
                            ins=[W2C, SW2], outs=[WL1b])
                        v.op(lambda ch=ch: nc.vector.tensor_scalar(
                            WL2b[:, :], W2C[:, ch % 2, :], SW2[:, 0:1], -0.5, A.mult, A.is_lt),
                            ins=[W2C, SW2], outs=[WL2b])
                        v.op(lambda ch=ch: nc.vector.tensor_tensor(
                            W2QB[:, ch % 2, :], WL1b[:, :], WL2b[:, :], A.subtract),
                            ins=[WL1b, WL2b], outs=[W2QB])
                        v.flush()
                        vec.engine_nop().then_inc(s_vp2, 1)
                    vec.nop().then_inc(bar_c, 1)

                @blkW2.scalar
                def _(sc):
                    sc.wait_ge(bar_b, 5)
                    sc.nop().then_inc(bar_c, 1)

                @blkW2.tensor
                def _(t):
                    t.wait_ge(bar_b, 5)
                    t.wait_ge(s_vp2, 9)
                    t.matmul(P2X[0:1, 0:1], WRED[:, 32:33], ONES1[:, :], start=True, stop=True).then_inc(s_mt, 1)
                    t.wait_ge(s_vp2, 10)
                    t.matmul(P2X[:, 0:2], ONF[0:1, :], SCR1[0:1, 0:2], start=True, stop=True).then_inc(s_tb, 1)
                    t.nop().then_inc(bar_c, 1)

            with ExitStack() as stB2:
                S = lambda *aa: stB2.enter_context(nc.sbuf_tensor(*aa))
                G2REP = S("g2rep", [128, H], BF16)
                BL2REP = S("bl2rep", [128, H], BF16)
                B2REP = S("b2rep", [128, D], BF16)
                H1B = S("h1b", [128, 2, H], BF16)
                U2 = S("u2", [128, 2, H], F32)
                QX2 = S("qx2", [128, H], BF16)
                QX2F = S("qx2f", [128, 32, 128], BF16)
                OUTF = S("outf", [128, 2, D], F32)
                RL = S("rl", [128, 2, H], F32)
                SGN = S("sgn", [128, 2, 2, 2048], BF16)
                L16b = S("l16b", [128, H], FP16) if ROUND_MODE == "fp16" else None
                PP2 = [stB2.enter_context(nc.psum_tensor(f"pp2{i}", [128, 1024], F32)) for i in range(2)]
                blkB2 = stB2.enter_context(nc.Block())

                ga_sem = lambda m: s_ga if m % 2 == 0 else s_ga2

                bh_sem = lambda n: s_bh if n % 2 == 0 else s_bh2
                bh_h1 = lambda n: 16 * (n // 2 + 1)

                @blkB2.sync
                def _(sync):
                    def tq_sub(m):
                        sync.wait_ge(s_vq2, m + 1)
                        if m >= 1:
                            sync.wait_ge(s_m2, m)
                        sync.dma_start_transpose(QX2F[:, :, :], QX2[:, :]).then_inc(s_tq2, 16)

                    sync.wait_ge(bar_c, 5)
                    for n in range(NT * NS):
                        if n >= 2:
                            sync.wait_ge(s_su2, n - 1)
                        sync.dma_start(
                            H1B[:, n % 2, :],
                            h1_h[(n // 4) * TT + (n % 4) * 128:(n // 4) * TT + (n % 4 + 1) * 128, :]
                        ).then_inc(bh_sem(n), 16)
                        if n >= 1:
                            tq_sub(n - 1)
                    tq_sub(NT * NS - 1)
                    sync.wait_ge(s_ga, 16 * (NT * NS // 2))
                    sync.wait_ge(s_ga2, 16 * (NT * NS // 2))
                    sync.nop()

                @blkB2.gpsimd
                def _(g):
                    gw = Eng(g)
                    g.wait_ge(bar_c, 5)
                    g.dma_start(G2REP[0:1, :], g2_h[0:1, :]).then_inc(s_gr, 16)
                    g.dma_start(BL2REP[0:1, :], bl2_h[0:1, :]).then_inc(s_gr, 16)
                    g.dma_start(B2REP[0:1, :], b2_h[0:1, :]).then_inc(s_gr, 16)
                    for m in range(NT * NS - 2):
                        g.wait_ge(s_ve2, m + 1)
                        g.dma_start(
                            out_h[(m // 4) * TT + (m % 4) * 128:(m // 4) * TT + (m % 4 + 1) * 128, :],
                            OUTF[:, m % 2, :], accum_op=A.add,
                        ).then_inc(ga_sem(m), 16)
                    for m in (NT * NS - 2, NT * NS - 1):
                        g.wait_ge(s_ve2, m + 1)
                        g.dma_start(
                            out_h[(m // 4) * TT + (m % 4) * 128:(m // 4) * TT + (m % 4 + 1) * 128, :],
                            OUTF[:, m % 2, :], accum_op=A.add,
                        ).then_inc(ga_sem(m), 16)
                    g.wait_ge(s_ga, 16 * (NT * NS // 2))
                    g.wait_ge(s_ga2, 16 * (NT * NS // 2))

                @blkB2.vector
                def _(vec):
                    v = Eng(vec)
                    vec.wait_ge(bar_c, 5)
                    vec.wait_ge(s_vgs, 1)
                    bt = [("g2", c) for c in range(8)] + [("bl2", c) for c in range(8)] + [("b2", c) for c in range(2)]
                    for k, (which, c) in enumerate(bt):
                        vec.wait_ge(s_tb, 9 + k)
                        if which == "g2":
                            v.op(lambda c=c: nc.vector.tensor_copy(
                                G2REP[:, 512 * c:512 * (c + 1)], PP2[0][:, 0:512]), ins=[PP2[0]], outs=[G2REP])
                        elif which == "bl2":
                            v.op(lambda c=c: nc.vector.tensor_copy(
                                BL2REP[:, 512 * c:512 * (c + 1)], PP2[0][:, 0:512]), ins=[PP2[0]], outs=[BL2REP])
                        else:
                            v.op(lambda c=c: nc.vector.tensor_copy(
                                B2REP[:, 512 * c:512 * (c + 1)], PP2[0][:, 0:512]), ins=[PP2[0]], outs=[B2REP])
                        v.flush()
                        vec.engine_nop().then_inc(s_vbc, 1)
                    def stats_tile(i):
                        c0 = 4 * i
                        h8 = 8 * (i % 2)
                        if i >= 2:
                            vec.wait_ge(s_su2, 4 * (i - 1))
                        v.op(lambda c0=c0: nc.vector.tensor_scalar(
                            TCOL[:, 0:4], ST_GS[:, c0:c0 + 4], 1.0 / 4096.0, None, A.mult),
                            ins=[ST_GS], outs=[TCOL])
                        v.op(lambda c0=c0: nc.vector.tensor_scalar(
                            TCOL[:, 4:8], ST_SQ[:, c0:c0 + 4], 1.0 / 4096.0, None, A.mult),
                            ins=[ST_SQ], outs=[TCOL])
                        v.op(lambda: nc.vector.tensor_tensor(TC2[:, 0:4], TCOL[:, 0:4], TCOL[:, 0:4], A.mult),
                             ins=[TCOL], outs=[TC2])
                        v.op(lambda: nc.vector.tensor_tensor(TC2[:, 4:8], TCOL[:, 4:8], TC2[:, 0:4], A.subtract),
                             ins=[TCOL, TC2], outs=[TC2])
                        v.op(lambda: nc.vector.tensor_scalar(WPK[:, 0:4], TC2[:, 4:8], 1e-5, None, A.add),
                             ins=[TC2], outs=[WPK])
                        _heron(v, nc, WPK, WPH, WPY, 4)
                        v.op(lambda h8=h8: nc.vector.reciprocal(TC3[:, h8:h8 + 4], WPY[:, 0:4]),
                             ins=[WPY], outs=[TC3])
                        v.op(lambda h8=h8: nc.vector.scalar_tensor_tensor(
                            TC3[:, h8 + 4:h8 + 8], TCOL[:, 0:4], -1.0, TC3[:, h8:h8 + 4], A.mult, A.mult),
                            ins=[TCOL, TC3], outs=[TC3])
                        v.flush()
                        vec.engine_nop().then_inc(s_vst2, 1)

                    def gam_sub(n):
                        vec.wait_ge(s_su2, n + 1)
                        v.op(lambda n=n: nc.vector.tensor_tensor(
                            U2[:, n % 2, :], U2[:, n % 2, :], G2REP[:, :], A.mult),
                            ins=[U2, G2REP], outs=[U2])
                        v.flush()

                    def bet_sub(n):
                        v.op(lambda n=n: nc.vector.tensor_tensor(
                            U2[:, n % 2, :], U2[:, n % 2, :], BL2REP[:, :], A.add),
                            ins=[U2, BL2REP], outs=[U2])
                        v.flush()

                    def scales_sub(n):
                        sl = n % 4
                        v.op(lambda sl=sl, n=n: nc.vector.reduce_max(
                            TCOL[:, sl:sl + 1], U2[:, n % 2, :], axis=AX, apply_absolute_value=True),
                            ins=[U2], outs=[TCOL])
                        v.op(lambda sl=sl: nc.vector.tensor_scalar(
                            TC2[:, sl:sl + 1], TCOL[:, sl:sl + 1], 1e-5, 1.0 / 127.0, A.max, A.mult),
                            ins=[TCOL], outs=[TC2])
                        v.op(lambda sl=sl, n=n: nc.vector.reciprocal(
                            ST_S2[:, n:n + 1], TC2[:, sl:sl + 1]), ins=[TC2], outs=[ST_S2])
                        v.op(lambda sl=sl, n=n: nc.vector.tensor_scalar(
                            ST_D2[:, n:n + 1], TC2[:, sl:sl + 1], SW2[:, 1:2], None, A.mult),
                            ins=[TC2, SW2], outs=[ST_D2])
                        v.flush()
                        vec.engine_nop().then_inc(s_vsc, 1)

                    def peel_sub(m, inj):
                        base = 16 * m
                        for b in range(8):
                            cb = _LAD_C[b]
                            for u in range(2):
                                vec.wait_ge(s_pg, base + 2 * b + u + 1)
                                v.op(lambda m=m, u=u, cb=cb: nc.vector.scalar_tensor_tensor(
                                    RL[:, m % 2, 2048 * u:2048 * (u + 1)], SGN[:, m % 2, u, :], -cb,
                                    RL[:, m % 2, 2048 * u:2048 * (u + 1)], A.mult, A.add),
                                    ins=[SGN, RL], outs=[RL])
                                v.flush()
                                vec.engine_nop().then_inc(s_lv, 1)
                            if b in inj:
                                inj[b]()

                    def finals_sub(m):
                        if m >= 1:
                            vec.wait_ge(s_tq2, 16 * m)
                        for u in range(2):
                            Rh = RL[:, m % 2, 2048 * u:2048 * (u + 1)]
                            v.op(lambda m=m, u=u, Rh=Rh: nc.vector.scalar_tensor_tensor(
                                QX2[:, 2048 * u:2048 * (u + 1)], U2[:, m % 2, 2048 * u:2048 * (u + 1)],
                                ST_S2[:, m:m + 1], Rh, A.mult, A.subtract),
                                ins=[U2, ST_S2, RL], outs=[QX2])
                        v.flush()
                        vec.engine_nop().then_inc(s_vq2, 1)

                    def outf_sub(m):
                        if m >= 2:
                            vec.wait_ge(ga_sem(m), 16 * (m // 2))
                        vec.wait_ge(s_m2, m + 1)
                        v.op(lambda m=m: nc.vector.scalar_tensor_tensor(
                            OUTF[:, m % 2, :], PP2[m % 2][:, :], ST_D2[:, m:m + 1],
                            B2REP[:, :], A.mult, A.add),
                            ins=[PP2[m % 2], ST_D2, B2REP], outs=[OUTF])
                        v.flush()
                        vec.engine_nop().then_inc(s_ve2, 1)

                    for n in range(NT * NS):
                        if n % 4 == 2 and n + 2 < NT * NS:
                            stats_tile((n + 2) // 4)
                        if n == 0:
                            stats_tile(0)
                            vec.wait_ge(s_vbc, 24)
                            gam_sub(0)
                            bet_sub(0)
                            scales_sub(0)
                        else:
                            inj = {2: (lambda n=n: gam_sub(n)),
                                   3: (lambda n=n: bet_sub(n)),
                                   4: (lambda n=n: scales_sub(n))}
                            if n >= 2:
                                inj[5] = (lambda n=n: outf_sub(n - 2))
                            peel_sub(n - 1, inj)
                            finals_sub(n - 1)
                    mL = NT * NS - 1
                    peel_sub(mL, {5: (lambda: outf_sub(mL - 1))})
                    finals_sub(mL)
                    outf_sub(mL)

                @blkB2.scalar
                def _(sc):
                    s_ = Eng(sc)
                    sc.wait_ge(bar_c, 5)

                    def emit_sign(m, b, u):
                        s_.op(lambda m=m, b=b, u=u: nc.scalar.activation(
                            SGN[:, m % 2, u, :], RL[:, m % 2, 2048 * u:2048 * (u + 1)], ACT.Sign,
                            bias=LADB[:, b:b + 1]),
                            ins=[RL, LADB], outs=[SGN])
                        s_.flush()
                        sc.nop().then_inc(s_pg, 1)

                    def z0_sub(m):
                        sc.wait_ge(s_vsc, m + 1)
                        s_.op(lambda m=m: nc.scalar.activation(
                            RL[:, m % 2, :], U2[:, m % 2, :], ACT.Identity,
                            bias=LADB[:, 9:10], scale=ST_S2[:, m:m + 1]),
                            ins=[U2, ST_S2, LADB], outs=[RL])
                        s_.flush()

                    def tail_signs(m):
                        for b in range(2, 8):
                            for u in range(2):
                                sc.wait_ge(s_lv, 16 * m + 2 * (b - 1) + u + 1)
                                emit_sign(m, b, u)
                            if b == 5 and m + 1 < NT * NS:
                                z0_sub(m + 1)
                        if m + 1 < NT * NS:
                            for u in range(2):
                                if m >= 1:
                                    sc.wait_ge(s_lv, 16 * (m - 1) + 15 + u)
                                emit_sign(m + 1, 0, u)

                    for n in range(NT * NS):
                        i, sl = divmod(n, 4)
                        m = n - 1
                        if n >= 1:
                            for u in range(2):
                                sc.wait_ge(s_lv, 16 * m + u + 1)
                                emit_sign(m, 1, u)
                        if sl == 0:
                            sc.wait_ge(s_vst2, i + 1)
                        sc.wait_ge(bh_sem(n), bh_h1(n))
                        if n >= 2:
                            sc.wait_ge(s_vq2, n - 1)
                        h8 = 8 * (i % 2)
                        s_.op(lambda sl=sl, n=n, h8=h8: nc.scalar.activation(
                            U2[:, n % 2, :], H1B[:, n % 2, :], ACT.Identity,
                            bias=TC3[:, h8 + 4 + sl:h8 + 5 + sl], scale=TC3[:, h8 + sl:h8 + sl + 1]),
                            ins=[H1B, TC3], outs=[U2])
                        s_.flush()
                        sc.nop().then_inc(s_su2, 1)
                        if n == 0:
                            z0_sub(0)
                            for u in range(2):
                                emit_sign(0, 0, u)
                        else:
                            tail_signs(m)
                    m = NT * NS - 1
                    for u in range(2):
                        sc.wait_ge(s_lv, 16 * m + u + 1)
                        emit_sign(m, 1, u)
                    tail_signs(m)

                @blkB2.tensor
                def _(t):
                    t.wait_ge(bar_c, 5)
                    t.wait_ge(s_gr, 48)
                    bt = [("g2", c) for c in range(8)] + [("bl2", c) for c in range(8)] + [("b2", c) for c in range(2)]
                    for k, (which, c) in enumerate(bt):
                        t.wait_ge(s_vbc, 8 + k)
                        if which == "g2":
                            t.matmul(PP2[0][:, 0:512], ONB[0:1, :], G2REP[0:1, 512 * c:512 * (c + 1)],
                                     start=True, stop=True).then_inc(s_tb, 1)
                        elif which == "bl2":
                            t.matmul(PP2[0][:, 0:512], ONB[0:1, :], BL2REP[0:1, 512 * c:512 * (c + 1)],
                                     start=True, stop=True).then_inc(s_tb, 1)
                        else:
                            t.matmul(PP2[0][:, 0:512], ONB[0:1, :], B2REP[0:1, 512 * c:512 * (c + 1)],
                                     start=True, stop=True).then_inc(s_tb, 1)
                    t.wait_ge(s_vbc, 8 + 18)
                    t.wait_ge(s_tw2w, 16 * 32)
                    for i in range(NT):
                        for s in range(NS):
                            n = 4 * i + s
                            t.wait_ge(s_tq2, 16 * (n + 1))
                            if n >= 2:
                                t.wait_ge(s_ve2, n - 1)
                            pp = PP2[n % 2]
                            ins = None
                            for half in range(2):
                                for hc in range(32):
                                    ins = t.matmul(pp[:, 512 * half:512 * (half + 1)],
                                                   QX2F[:, hc, :],
                                                   W2T[:, hc, 512 * half:512 * (half + 1)],
                                                   start=(hc == 0), stop=(hc == 31))
                            ins.then_inc(s_m2, 1)

    return nc


_CACHE = {}


def kernel(L0_attn_x, L0_attn_gamma, L0_attn_w_qkva, L0_ffn_ln1_g, L0_ffn_ln1_b,
           L0_ffn_w1, L0_ffn_b1, L0_ffn_ln2_g, L0_ffn_ln2_b, L0_ffn_w2, L0_ffn_b2):
    if "nc" not in _CACHE:
        _CACHE["nc"] = build_program()
    nc = _CACHE["nc"]
    x = np.ascontiguousarray(np.asarray(L0_attn_x, dtype=np.float32))
    row = lambda a: np.ascontiguousarray(np.asarray(a, dtype=np.float32).reshape(1, -1))
    mat = lambda a: np.ascontiguousarray(np.asarray(a, dtype=np.float32))
    base = {
        "wqkva": mat(L0_attn_w_qkva),
        "gamma": row(L0_attn_gamma),
        "ln1g": row(L0_ffn_ln1_g),
        "ln1b": row(L0_ffn_ln1_b),
        "w1": mat(L0_ffn_w1),
        "b1": row(L0_ffn_b1),
        "ln2g": row(L0_ffn_ln2_g),
        "ln2b": row(L0_ffn_ln2_b),
        "w2": mat(L0_ffn_w2),
        "b2": row(L0_ffn_b2),
    }
    in_maps = [dict(base, x=np.ascontiguousarray(x[c])) for c in range(N_CORES)]
    res = run_bass_kernel_spmd(nc, in_maps, core_ids=list(range(N_CORES)))
    out = np.stack([np.asarray(res.results[c]["out"]) for c in range(N_CORES)], axis=0)
    return out.astype(np.float32)


# version 30 (final)
# speedup vs baseline: 1.0448x; 1.0448x over previous
import numpy as np
from contextlib import ExitStack

import concourse.bass as bass
import concourse.mybir as mybir
from concourse.mybir import AluOpType as A, ActivationFunctionType as ACT
from concourse.bass_utils import run_bass_kernel_spmd
import ml_dtypes

F32 = mybir.dt.float32
BF16 = mybir.dt.bfloat16
FP16 = mybir.dt.float16
I32 = mybir.dt.int32
AX = mybir.AxisListType.X

ROUND_MODE = "ladder"

N_CORES = 8
SEQ = 4096
D = 1024
H = 4096
E3 = 3072
NT = 8
TT = 512
NS = 4
HERON_ITERS = 16


class Eng:

    def __init__(self, e):
        self.e, self.w, self.r = e, set(), set()

    @staticmethod
    def _k(ap):
        t = getattr(ap, "tensor", None)
        return id(t) if t is not None else id(ap)

    def op(self, emitfn, ins=(), outs=()):
        ki = {self._k(a) for a in ins}
        ko = {self._k(a) for a in outs}
        if (ki & self.w) or (ko & (self.w | self.r)):
            self.e.drain()
            self.w.clear()
            self.r.clear()
        res = emitfn()
        self.w |= ko
        self.r |= ki
        return res

    def flush(self):
        self.e.drain()
        self.w.clear()
        self.r.clear()


def _heron(v, nc, W, WH, Y, n):
    v.op(lambda: nc.vector.tensor_scalar(WH[:, :n], W[:, :n], 0.5, None, A.mult),
         ins=[W], outs=[WH])
    v.op(lambda: nc.vector.memset(Y[:, :n], 1.0), outs=[Y])
    for _ in range(HERON_ITERS):
        v.op(lambda: nc.vector.reciprocal(W[:, :n], Y[:, :n]), ins=[Y], outs=[W])
        v.op(lambda: nc.vector.tensor_tensor(W[:, :n], WH[:, :n], W[:, :n], A.mult),
             ins=[WH, W], outs=[W])
        v.op(lambda: nc.vector.scalar_tensor_tensor(Y[:, :n], Y[:, :n], 0.5, W[:, :n], A.mult, A.add),
             ins=[Y, W], outs=[Y])


_LAD_C = [float(128 >> k) / 2.0 for k in range(8)]
_LAD_BIAS = []
_acc = 0.0
for _k in range(8):
    _LAD_BIAS.append(-(float(128 >> _k) + _acc))
    _acc += _LAD_C[_k]


def _round_q(v, nc, src, scale_ap, out_bf, L0, LP, LB, L16):
    v.op(lambda: nc.vector.tensor_scalar(L16, src, scale_ap, 1536.0, A.mult, A.add),
         ins=[src, scale_ap], outs=[L16])
    v.op(lambda: nc.vector.tensor_scalar(out_bf, L16, -1536.0, None, A.add),
         ins=[L16], outs=[out_bf])


def _ladgrp_vec(v, nc, vec, g, srcs, scale_ap, outs, Z, W, SGN, s_ls, s_lv):
    for u in range(2):
        v.op(lambda u=u: nc.vector.tensor_scalar(Z[:, u, :], srcs[u], scale_ap, 128.5, A.mult, A.add),
             ins=[srcs[u], scale_ap], outs=[Z])
        v.op(lambda u=u: nc.vector.tensor_copy(W[:, u, :], Z[:, u, :]), ins=[Z], outs=[W])
        v.flush()
        vec.engine_nop().then_inc(s_lv, 1)
    for b in range(8):
        for u in range(2):
            vec.wait_ge(s_ls, 16 * g + 2 * b + u + 1)
            v.op(lambda u=u, b=b: nc.vector.scalar_tensor_tensor(
                W[:, u, :], SGN[:, u, :], -_LAD_C[b], W[:, u, :], A.mult, A.add),
                ins=[SGN, W], outs=[W])
            v.flush()
            vec.engine_nop().then_inc(s_lv, 1)
    for u in range(2):
        v.op(lambda u=u: nc.vector.scalar_tensor_tensor(
            Z[:, u, :], W[:, u, :], -1.0, Z[:, u, :], A.mult, A.add), ins=[W, Z], outs=[Z])
        v.op(lambda u=u: nc.vector.tensor_scalar(outs[u], Z[:, u, :], -0.5, None, A.add),
             ins=[Z], outs=[outs[u]])


def _ladgrp_scl(s_, nc, sc, g, W, SGN, LADB, s_ls, s_lv):
    for b in range(8):
        for u in range(2):
            if b == 0:
                sc.wait_ge(s_lv, 18 * g + 1 + u)
            else:
                sc.wait_ge(s_lv, 18 * g + 2 * b + 1 + u)
            s_.op(lambda u=u, b=b: nc.scalar.activation(
                SGN[:, u, :], W[:, u, :], ACT.Sign, bias=LADB[:, b:b + 1]),
                ins=[W, LADB], outs=[SGN])
            s_.flush()
            sc.nop().then_inc(s_ls, 1)


def _ladder_vonly(v, nc, src, scale_ap, out_bf, Z, P, B):
    v.op(lambda: nc.vector.tensor_scalar(Z, src, scale_ap, 128.5, A.mult, A.add),
         ins=[src, scale_ap], outs=[Z])
    for k in range(8):
        th = float(128 >> k)
        cur = Z if k == 0 else P
        v.op(lambda cur=cur, th=th: nc.vector.tensor_scalar(B, cur, th, -th, A.is_ge, A.mult),
             ins=[cur], outs=[B])
        v.op(lambda cur=cur: nc.vector.tensor_tensor(P, cur, B, A.add), ins=[cur, B], outs=[P])
    v.op(lambda: nc.vector.tensor_tensor(Z, Z, P, A.subtract), ins=[Z, P], outs=[Z])
    v.op(lambda: nc.vector.tensor_scalar(out_bf, Z, -128.0, None, A.add), ins=[Z], outs=[out_bf])


def _ladgrp4_vec(v, nc, vec, g, srcs, scale_ap, outs, Z, W, SGN, s_ls, s_lv):
    for u in range(4):
        v.op(lambda u=u: nc.vector.tensor_scalar(Z[:, u, :], srcs[u], scale_ap, 128.5, A.mult, A.add),
             ins=[srcs[u], scale_ap], outs=[Z])
        v.op(lambda u=u: nc.vector.tensor_copy(W[:, u, :], Z[:, u, :]), ins=[Z], outs=[W])
        v.flush()
        vec.engine_nop().then_inc(s_lv, 1)
    for b in range(8):
        for u in range(4):
            vec.wait_ge(s_ls, 32 * g + 4 * b + u + 1)
            v.op(lambda u=u, b=b: nc.vector.scalar_tensor_tensor(
                W[:, u, :], SGN[:, u, :], -_LAD_C[b], W[:, u, :], A.mult, A.add),
                ins=[SGN, W], outs=[W])
            v.flush()
            vec.engine_nop().then_inc(s_lv, 1)
    for u in range(4):
        v.op(lambda u=u: nc.vector.scalar_tensor_tensor(
            Z[:, u, :], W[:, u, :], -1.0, Z[:, u, :], A.mult, A.add), ins=[W, Z], outs=[Z])
        v.op(lambda u=u: nc.vector.tensor_scalar(outs[u], Z[:, u, :], -0.5, None, A.add),
             ins=[Z], outs=[outs[u]])


def _ladgrp4_scl(s_, nc, sc, g, W, SGN, LADB, s_ls, s_lv):
    for b in range(8):
        for u in range(4):
            if b == 0:
                sc.wait_ge(s_lv, 36 * g + u + 1)
            else:
                sc.wait_ge(s_lv, 36 * g + 4 * b + u + 1)
            s_.op(lambda u=u, b=b: nc.scalar.activation(
                SGN[:, u, :], W[:, u, :], ACT.Sign, bias=LADB[:, b:b + 1]),
                ins=[W, LADB], outs=[SGN])
            s_.flush()
            sc.nop().then_inc(s_ls, 1)


def build_program():
    nc = bass.Bass("TRN2", target_bir_lowering=False)

    x_h = nc.dram_tensor("x", [SEQ, D], F32, kind="ExternalInput").ap()
    wq_h = nc.dram_tensor("wqkva", [E3, D], F32, kind="ExternalInput").ap()
    gam_h = nc.dram_tensor("gamma", [1, D], F32, kind="ExternalInput").ap()
    g1_h = nc.dram_tensor("ln1g", [1, D], F32, kind="ExternalInput").ap()
    bl1_h = nc.dram_tensor("ln1b", [1, D], F32, kind="ExternalInput").ap()
    w1_h = nc.dram_tensor("w1", [H, D], F32, kind="ExternalInput").ap()
    b1_h = nc.dram_tensor("b1", [1, H], F32, kind="ExternalInput").ap()
    g2_h = nc.dram_tensor("ln2g", [1, H], F32, kind="ExternalInput").ap()
    bl2_h = nc.dram_tensor("ln2b", [1, H], F32, kind="ExternalInput").ap()
    w2_h = nc.dram_tensor("w2", [D, H], F32, kind="ExternalInput").ap()
    b2_h = nc.dram_tensor("b2", [1, D], F32, kind="ExternalInput").ap()
    out_h = nc.dram_tensor("out", [SEQ, D], F32, kind="ExternalOutput").ap()
    w1q_h = nc.dram_tensor("w1q_scr", [H, D], BF16, kind="Internal").ap()
    w2q_h = nc.dram_tensor("w2q_scr", [D, H], BF16, kind="Internal").ap()
    h1_h = nc.dram_tensor("h1_scr", [SEQ, H], BF16, kind="Internal").ap()
    identc_h = nc.inline_tensor(np.eye(128, dtype=ml_dtypes.bfloat16), name="identc").ap()
    onesf_h = nc.inline_tensor(np.ones((1, 128), dtype=np.float32), name="onesf").ap()
    onesb_h = nc.inline_tensor(np.ones((1, 128), dtype=ml_dtypes.bfloat16), name="onesb").ap()

    with ExitStack() as outer:
        SEM = lambda name: outer.enter_context(nc.semaphore(name))
        s_dx0 = SEM("s_dx0"); s_dx1 = SEM("s_dx1")
        s_twq = SEM("s_twq"); s_txr = SEM("s_txr"); s_tat = SEM("s_tat")
        s_so = SEM("s_so")
        s_dga = SEM("s_dga"); s_dw0 = SEM("s_dw0"); s_dw1 = SEM("s_dw1")
        s_gb1r = SEM("s_gb1r"); s_gr = SEM("s_gr"); s_ga = SEM("s_ga")
        s_gw0 = SEM("s_gw0"); s_gw1 = SEM("s_gw1")
        s_gx0 = SEM("s_gx0"); s_gx1 = SEM("s_gx1")
        s_gs0 = SEM("s_gs0"); s_gs1 = SEM("s_gs1")
        s_br1 = SEM("s_br1"); s_bxl = SEM("s_bxl")
        s_tw1w = SEM("s_tw1w"); s_tq1 = SEM("s_tq1")
        s_tw2w = SEM("s_tw2w"); s_tq2 = SEM("s_tq2")
        s_tb = SEM("s_tb"); s_vbc = SEM("s_vbc")
        s_ls = SEM("s_ls"); s_lv = SEM("s_lv")
        s_pg = SEM("s_pg"); s_vsc = SEM("s_vsc"); s_sr = SEM("s_sr"); s_bh2 = SEM("s_bh2"); s_ga2 = SEM("s_ga2"); s_tq1b = SEM("s_tq1b"); s_bxl2 = SEM("s_bxl2"); s_sh1b = SEM("s_sh1b"); s_so2 = SEM("s_so2"); s_so3 = SEM("s_so3"); s_so4 = SEM("s_so4"); s_dx2 = SEM("s_dx2"); s_dx3 = SEM("s_dx3")
        s_vs = SEM("s_vs");  s_vw = SEM("s_vw");  s_ve = SEM("s_ve");  s_va = SEM("s_va")
        s_vc = SEM("s_vc");  s_sx = SEM("s_sx");  s_se = SEM("s_se");  s_mq = SEM("s_mq")
        bar_a = SEM("bar_a")
        s_bx = SEM("s_bx");  s_tw1 = SEM("s_tw1"); s_sh1 = SEM("s_sh1")
        s_su = SEM("s_su");  s_sg = SEM("s_sg");  s_ss = SEM("s_ss")
        s_vq = SEM("s_vq");  s_vd = SEM("s_vd");  s_vr = SEM("s_vr");  s_vp = SEM("s_vp")
        s_vgs = SEM("s_vgs")
        s_m1 = SEM("s_m1");  s_mt = SEM("s_mt")
        bar_b = SEM("bar_b")
        s_bh = SEM("s_bh");  s_tw2 = SEM("s_tw2")
        s_su2 = SEM("s_su2"); s_vq2 = SEM("s_vq2"); s_ve2 = SEM("s_ve2"); s_vp2 = SEM("s_vp2")
        s_m2 = SEM("s_m2");  s_vst2 = SEM("s_vst2")
        bar_c = SEM("bar_c")

        SB = lambda *aa: outer.enter_context(nc.sbuf_tensor(*aa))
        ST_CA = SB("st_ca", [128, 32], F32); ST_CB = SB("st_cb", [128, 32], F32)
        ST_E = SB("st_e", [128, 32], F32);   ST_F = SB("st_f", [128, 32], F32)
        ST_GS = SB("st_gs", [128, 32], F32); ST_SQ = SB("st_sq", [128, 32], F32)
        ST_S1 = SB("st_s1", [128, 32], F32); ST_D1 = SB("st_d1", [128, 32], F32)
        ST_S2 = SB("st_s2", [128, 32], F32); ST_D2 = SB("st_d2", [128, 32], F32)
        SW1 = SB("sw1", [128, 2], F32); SW2 = SB("sw2", [128, 2], F32)
        WRED = SB("wred", [128, 34], F32)
        ONES1 = SB("ones1", [128, 1], F32)
        SCR1 = SB("scr1", [1, 4], F32)
        DQ1R = SB("dq1r", [1, 512], BF16)
        DQ1B = SB("dq1b", [128, 4], BF16)
        STATE = SB("state", [128, 8], F32)
        BNS = SB("bns", [128, 12], F32)
        MV = SB("mv", [128, 8], F32)
        WPK = SB("wpk", [128, 12], F32); WPH = SB("wph", [128, 12], F32); WPY = SB("wpy", [128, 12], F32)
        TCOL = SB("tcol", [128, 8], F32); TC2 = SB("tc2", [128, 8], F32); TC3 = SB("tc3", [128, 16], F32)
        IDENT = SB("ident", [128, 128], BF16)
        ONF = SB("onf", [1, 128], F32)
        ONB = SB("onb", [1, 128], BF16)
        LADB = SB("ladb", [128, 12], F32)

        g_cnt = [0]

        with ExitStack() as stA:
            S = lambda *aa: stA.enter_context(nc.sbuf_tensor(*aa))
            GREP = S("grep", [128, D], F32)
            WQC = S("wqc", [128, 2, D], F32)
            WGB = S("wgb", [128, 2, D], BF16)
            WQT = S("wqt", [128, 8, E3], BF16)
            XT = S("xt", [128, 3, NS, D], F32)
            XR = S("xr", [128, NS, D], BF16)
            XRF = S("xrf", [128, 8, 2 * TT], BF16)
            QF = S("qf", [128, 8, TT], BF16)
            KF = S("kf", [128, 8, TT], BF16)
            AF = S("af", [128, 8, TT], F32)
            KS = S("ks", [128, 8, TT], F32)
            ATB = S("atb", [128, NS, 8, 128], BF16)
            ATM = S("atm", [128, NS, D], BF16)
            PQ = [stA.enter_context(nc.psum_tensor(f"pq{i}", [128, TT], F32)) for i in range(2)]
            blkA = stA.enter_context(nc.Block())


            dx_sems = [s_dx0, s_dx1, s_dx2, s_dx3]
            so_sems = [s_so, s_so2, s_so3, s_so4]

            def ve_wait(m):
                t, j = divmod(m, 24)
                if j < 8:
                    return s_se, 8 * t + j + 1
                if j < 16:
                    return s_ve, 16 * t + (j - 8) + 1
                return s_ve, 16 * t + (j - 7)

            @blkA.sync
            def _(sync):
                sync.dma_start(GREP[0:1, :], gam_h[0:1, :]).then_inc(s_dga, 16)
                sync.dma_start(IDENT[:, :], identc_h[:, :]).then_inc(s_dga, 16)
                sync.dma_start(ONF[0:1, :], onesf_h[0:1, :]).then_inc(s_dga, 16)
                sync.dma_start(ONB[0:1, :], onesb_h[0:1, :]).then_inc(s_dga, 16)
                for e in range(24):
                    if e >= 2:
                        sync.wait_ge(s_vw, e - 1)
                    sync.dma_start(WQC[:, e % 2, :], wq_h[e * 128:(e + 1) * 128, :]).then_inc(
                        s_dw0 if e % 2 == 0 else s_dw1, 16)
                    if e >= 1:
                        sync.wait_ge(s_vw, e)
                        sync.dma_start_transpose(
                            WQT[:, :, (e - 1) * 128:e * 128], WGB[:, (e - 1) % 2, :]
                        ).then_inc(s_twq, 16)
                sync.wait_ge(s_vw, 24)
                sync.dma_start_transpose(WQT[:, :, 23 * 128:24 * 128], WGB[:, 1, :]).then_inc(s_twq, 16)
                for i in range(NT):
                    sync.wait_ge(s_sx, i + 1)
                    if i >= 2:
                        sync.wait_ge(s_mq, 24 * (i - 1))
                    for s in range(NS):
                        sync.dma_start_transpose(
                            XRF[:, :, (i % 2) * TT + s * 128:(i % 2) * TT + (s + 1) * 128], XR[:, s, :]
                        ).then_inc(s_txr, 16)
                    if i >= 1:
                        sync.wait_ge(s_va, i)
                        if i >= 2:
                            sync.wait_ge(s_vc, 4 * (i - 1))
                        for s in range(NS):
                            sync.dma_start_transpose(
                                ATM[:, s, :].rearrange("p (c i) -> p c i", c=8),
                                ATB[:, s, :, :],
                            ).then_inc(s_tat, 16)
                sync.wait_ge(s_va, NT)
                sync.wait_ge(s_vc, 4 * (NT - 1))
                for s in range(NS):
                    sync.dma_start_transpose(
                        ATM[:, s, :].rearrange("p (c i) -> p c i", c=8),
                        ATB[:, s, :, :],
                    ).then_inc(s_tat, 16)
                sync.wait_ge(s_twq, 16 * 24)
                sync.wait_ge(s_txr, 16 * 4 * NT)
                sync.wait_ge(s_tat, 16 * 4 * NT)
                sync.nop().then_inc(bar_a, 1)

            @blkA.gpsimd
            def _(g):
                for j in (0, 1, 2):
                    for s in range(NS):
                        g.dma_start(
                            XT[:, j % 3, s, :],
                            x_h[j * TT + s * 128:j * TT + (s + 1) * 128, :],
                        ).then_inc(s_sr, 16)
                for i in range(1, NT):
                    for s in range(NS):
                        g.wait_ge(s_vc, 4 * (i - 1) + s + 1)
                        g.dma_start(
                            out_h[(i - 1) * TT + s * 128:(i - 1) * TT + (s + 1) * 128, :],
                            XT[:, (i - 1) % 3, s, :],
                        ).then_inc(so_sems[s], 16)
                        if i + 2 < NT:
                            g.wait_ge(so_sems[s], 16 * i)
                            g.dma_start(
                                XT[:, (i + 2) % 3, s, :],
                                x_h[(i + 2) * TT + s * 128:(i + 2) * TT + (s + 1) * 128, :],
                            ).then_inc(dx_sems[s], 16)
                for s in range(NS):
                    g.wait_ge(s_vc, 4 * (NT - 1) + s + 1)
                    g.dma_start(
                        out_h[(NT - 1) * TT + s * 128:(NT - 1) * TT + (s + 1) * 128, :],
                        XT[:, (NT - 1) % 3, s, :],
                    ).then_inc(so_sems[s], 16)
                for s in range(NS):
                    g.wait_ge(so_sems[s], 16 * NT)
                g.nop().then_inc(bar_a, 1)

            @blkA.vector
            def _(vec):
                v = Eng(vec)
                v.op(lambda: nc.vector.memset(ONES1[:, :], 1.0), outs=[ONES1])
                for b in range(8):
                    v.op(lambda b=b: nc.vector.memset(LADB[:, b:b + 1], _LAD_BIAS[b] + 128.0), outs=[LADB])
                v.op(lambda: nc.vector.memset(LADB[:, 8:9], 128.5), outs=[LADB])
                v.op(lambda: nc.vector.memset(LADB[:, 9:10], 0.5), outs=[LADB])
                v.op(lambda: nc.vector.memset(STATE[:, :], 0.0), outs=[STATE])
                for c in range(2):
                    vec.wait_ge(s_tb, c + 1)
                    v.op(lambda c=c: nc.vector.tensor_copy(GREP[:, 512 * c:512 * (c + 1)], PQ[0][:, :]),
                         ins=[PQ[0]], outs=[GREP])
                    v.flush()
                    vec.engine_nop().then_inc(s_vbc, 1)
                for e in range(24):
                    vec.wait_ge(s_dw0 if e % 2 == 0 else s_dw1, 16 * (e // 2 + 1))
                    if e >= 2:
                        vec.wait_ge(s_twq, 16 * e)
                    v.op(lambda e=e: nc.vector.tensor_tensor(
                        WGB[:, e % 2, :], WQC[:, e % 2, :], GREP[:, :], A.mult),
                        ins=[WQC, GREP], outs=[WGB])
                    v.flush()
                    vec.engine_nop().then_inc(s_vw, 1)
                def emit_stats(j):
                    p = j % 3
                    c0 = 4 * j
                    if j <= 2:
                        vec.wait_ge(s_sr, 16 * 12)
                    for s in range(NS):
                        if j >= 3:
                            vec.wait_ge(dx_sems[s], 16 * (j - 2))
                        v.op(lambda s=s, p=p: nc.vector.bn_stats(BNS[:, 0:6], XT[:, p, s, 0:512]),
                             ins=[XT], outs=[BNS])
                        v.op(lambda s=s, p=p: nc.vector.bn_stats(BNS[:, 6:12], XT[:, p, s, 512:1024]),
                             ins=[XT], outs=[BNS])
                        v.op(lambda s=s, p=p: nc.vector.bn_aggr(MV[:, 2 * s:2 * s + 2], BNS[:, :]),
                             ins=[BNS], outs=[MV])
                    v.op(lambda: nc.vector.tensor_copy(TCOL[:, 0:4], MV[:, 0:8:2]), ins=[MV], outs=[TCOL])
                    v.op(lambda: nc.vector.tensor_copy(TCOL[:, 4:8], MV[:, 1:8:2]), ins=[MV], outs=[TCOL])
                    v.op(lambda: nc.vector.tensor_scalar(WPK[:, 0:4], TCOL[:, 4:8], 1e-5, None, A.add),
                         ins=[TCOL], outs=[WPK])
                    v.op(lambda: nc.vector.reciprocal(TC2[:, 0:4], WPK[:, 0:4]), ins=[WPK], outs=[TC2])
                    v.op(lambda: nc.vector.tensor_tensor(TC2[:, 4:8], TCOL[:, 4:8], TC2[:, 0:4], A.mult),
                         ins=[TCOL, TC2], outs=[TC2])
                    v.op(lambda: nc.vector.tensor_scalar(WPK[:, 4:8], TC2[:, 4:8], 1e-5, None, A.add),
                         ins=[TC2], outs=[WPK])
                    v.op(lambda: nc.vector.tensor_scalar(WPK[:, 8:12], TC2[:, 4:8], 1024.0, None, A.mult),
                         ins=[TC2], outs=[WPK])
                    _heron(v, nc, WPK, WPH, WPY, 12)
                    v.op(lambda: nc.vector.reciprocal(TC3[:, 0:4], WPY[:, 0:4]), ins=[WPY], outs=[TC3])
                    v.op(lambda: nc.vector.reciprocal(TC3[:, 4:8], WPY[:, 4:8]), ins=[WPY], outs=[TC3])
                    v.op(lambda: nc.vector.tensor_scalar(WPY[:, 8:12], WPY[:, 8:12], 1e-12, None, A.max),
                         ins=[WPY], outs=[WPY])
                    v.op(lambda: nc.vector.reciprocal(TC2[:, 0:4], WPY[:, 8:12]), ins=[WPY], outs=[TC2])
                    v.op(lambda: nc.vector.tensor_scalar(TC2[:, 0:4], TC2[:, 0:4], 32.0, None, A.mult),
                         ins=[TC2], outs=[TC2])
                    v.op(lambda c0=c0: nc.vector.tensor_tensor(
                        ST_CA[:, c0:c0 + 4], TC3[:, 0:4], TC2[:, 0:4], A.mult),
                        ins=[TC3, TC2], outs=[ST_CA])
                    v.op(lambda c0=c0: nc.vector.scalar_tensor_tensor(
                        ST_CB[:, c0:c0 + 4], TCOL[:, 0:4], -1.0, ST_CA[:, c0:c0 + 4], A.mult, A.mult),
                        ins=[TCOL, ST_CA], outs=[ST_CB])
                    v.op(lambda c0=c0: nc.vector.tensor_tensor(
                        ST_E[:, c0:c0 + 4], TC3[:, 0:4], TC3[:, 4:8], A.mult),
                        ins=[TC3], outs=[ST_E])
                    v.op(lambda c0=c0: nc.vector.scalar_tensor_tensor(
                        ST_F[:, c0:c0 + 4], TCOL[:, 0:4], -1.0, ST_E[:, c0:c0 + 4], A.mult, A.mult),
                        ins=[TCOL, ST_E], outs=[ST_F])
                    v.flush()
                    vec.engine_nop().then_inc(s_vs, 1)

                emit_stats(0)
                for i in range(NT):
                    p = i % 3
                    c0 = 4 * i
                    if i + 1 < NT:
                        emit_stats(i + 1)
                    if i >= 1:
                        vec.wait_ge(s_tat, 64 * i)
                        for s in range(NS):
                            v.op(lambda s=s, i=i: nc.vector.tensor_tensor(
                                XT[:, (i - 1) % 3, s, :], XT[:, (i - 1) % 3, s, :],
                                ATM[:, s, :], A.add), ins=[XT, ATM], outs=[XT])
                            v.flush()
                            vec.engine_nop().then_inc(s_vc, 1)
                    for dc in range(8):
                        m = 24 * i + 8 + dc
                        vec.wait_ge(s_mq, m + 1)
                        v.op(lambda dc=dc, m=m: nc.vector.tensor_copy(KF[:, dc, :], PQ[m % 2][:, :]),
                             ins=[PQ[m % 2]], outs=[KF])
                        v.flush()
                        vec.engine_nop().then_inc(s_ve, 1)
                        vec.wait_ge(s_se, 8 * i + dc + 1)
                        v.op(lambda dc=dc: nc.vector.tensor_tensor_scan(
                            KS[:, dc, :], AF[:, dc, :], KF[:, dc, :], STATE[:, dc:dc + 1], A.mult, A.add),
                            ins=[AF, KF, STATE], outs=[KS])
                        v.op(lambda dc=dc: nc.vector.tensor_copy(STATE[:, dc:dc + 1], KS[:, dc, TT - 1:TT]),
                             ins=[KS], outs=[STATE])
                    for dc in range(8):
                        m = 24 * i + 16 + dc
                        vec.wait_ge(s_mq, m + 1)
                        v.op(lambda dc=dc, m=m: nc.vector.tensor_copy(QF[:, dc, :], PQ[m % 2][:, :]),
                             ins=[PQ[m % 2]], outs=[QF])
                        v.flush()
                        vec.engine_nop().then_inc(s_ve, 1)
                    v.op(lambda: nc.vector.tensor_tensor(
                        ATB[:, :, :, :].rearrange("p s c j -> p c s j"),
                        QF[:, :, :].rearrange("p c (s j) -> p c s j", s=NS),
                        KS[:, :, :].rearrange("p c (s j) -> p c s j", s=NS), A.mult),
                         ins=[QF, KS], outs=[ATB])
                    v.flush()
                    vec.engine_nop().then_inc(s_va, 1)
                vec.wait_ge(s_tat, 64 * NT)
                for s in range(NS):
                    v.op(lambda s=s: nc.vector.tensor_tensor(
                        XT[:, (NT - 1) % 3, s, :], XT[:, (NT - 1) % 3, s, :],
                        ATM[:, s, :], A.add), ins=[XT, ATM], outs=[XT])
                    v.flush()
                    vec.engine_nop().then_inc(s_vc, 1)
                vec.nop().then_inc(bar_a, 1)

            @blkA.scalar
            def _(sc):
                s_ = Eng(sc)
                for i in range(NT):
                    p = i % 3
                    c0 = 4 * i
                    sc.wait_ge(s_vs, i + 1)
                    if i >= 1:
                        sc.wait_ge(s_txr, 16 * 4 * i)
                    for s in range(NS):
                        s_.op(lambda s=s, c0=c0, p=p: nc.scalar.activation(
                            XR[:, s, :], XT[:, p, s, :], ACT.Identity,
                            bias=ST_CB[:, c0 + s:c0 + s + 1], scale=ST_CA[:, c0 + s:c0 + s + 1]),
                            ins=[XT, ST_CA, ST_CB], outs=[XR])
                    s_.flush()
                    sc.nop().then_inc(s_sx, 1)
                    if i >= 1:
                        sc.wait_ge(s_va, i)
                    for j in range(8):
                        m = 24 * i + j
                        sc.wait_ge(s_mq, m + 1)
                        s_.op(lambda j=j, m=m: nc.scalar.activation(
                            AF[:, j, :], PQ[m % 2][:, :], ACT.Sigmoid),
                            ins=[PQ[m % 2]], outs=[AF])
                        s_.flush()
                        sc.nop().then_inc(s_se, 1)
                sc.nop().then_inc(bar_a, 1)

            @blkA.tensor
            def _(t):
                t.wait_ge(s_dga, 64)
                for c in range(2):
                    if c >= 1:
                        t.wait_ge(s_vbc, c)
                    t.matmul(PQ[0][:, :], ONF[0:1, :], GREP[0:1, 512 * c:512 * (c + 1)],
                             start=True, stop=True).then_inc(s_tb, 1)
                t.wait_ge(s_vbc, 2)
                if True:
                    t.wait_ge(s_twq, 16 * 24)
                elist = list(range(16, 24)) + list(range(8, 16)) + list(range(0, 8))
                for i in range(NT):
                    t.wait_ge(s_txr, 16 * 4 * (i + 1))
                    for j, e in enumerate(elist):
                        m = 24 * i + j
                        if m >= 2:
                            sem, val = ve_wait(m - 2)
                            t.wait_ge(sem, val)
                        ins = None
                        for dc in range(8):
                            ins = t.matmul(
                                PQ[m % 2][:, :],
                                WQT[:, dc, e * 128:(e + 1) * 128],
                                XRF[:, dc, (i % 2) * TT:(i % 2 + 1) * TT],
                                start=(dc == 0), stop=(dc == 7))
                        ins.then_inc(s_mq, 1)
                t.nop().then_inc(bar_a, 1)

        with ExitStack() as stB1:
            S = lambda *aa: stB1.enter_context(nc.sbuf_tensor(*aa))
            G1REP = S("g1rep", [128, D], F32)
            BL1REP = S("bl1rep", [128, D], F32)
            B1ROW = S("b1row", [1, H], BF16)
            W1C = S("w1c", [128, 2, D], F32)
            W1QB = S("w1qb", [128, 2, D], BF16)
            W1T = S("w1t", [128, 8, H], BF16)
            XBT = S("xbt", [128, NS, D], F32)
            U1 = S("u1", [128, 2, D], F32)
            QX1 = S("qx1", [128, 2, D], BF16)
            XF1 = S("xf1", [128, 8, 256], BF16)
            H1T = S("h1t", [128, NS, H], BF16)
            SQD = S("sqd", [128, H], BF16)
            GT4 = S("gt4", [128, 32, 4], F32)
            R1 = S("r1", [128, 2, D], F32)
            B1L = S("b1l", [128, D], F32)
            L16 = S("l16", [128, D], FP16) if ROUND_MODE == "fp16" else None
            PP1 = [stB1.enter_context(nc.psum_tensor(f"pp1{i}", [128, 1024], F32)) for i in range(2)]
            PB1DQ = stB1.enter_context(nc.psum_tensor("pb1dq", [128, 512], BF16))
            P1X = stB1.enter_context(nc.psum_tensor("p1x", [128, 2], F32))
            blkB1 = stB1.enter_context(nc.Block())

            gw_l = lambda ch: 16 * (ch // 2 + 1)
            gs_st = lambda ch: 16 * (ch // 2 + 1)
            bx_x = lambda i: 16 * (i + 1)

            @blkB1.sync
            def _(sync):
                sync.wait_ge(bar_a, 5)
                sync.dma_start(G1REP[0:1, :], g1_h[0:1, :]).then_inc(s_br1, 16)
                sync.dma_start(BL1REP[0:1, :], bl1_h[0:1, :]).then_inc(s_br1, 16)
                sync.wait_ge(s_gs0, 16 * 16)
                sync.wait_ge(s_gs1, 16 * 16)
                for dc in range(8):
                    sync.dma_start_transpose(W1T[:, dc, :], w1q_h[:, dc * 128:(dc + 1) * 128]).then_inc(s_tw1w, 16)

                def st_sub(m):
                    sync.wait_ge(s_ss, m + 1)
                    sync.dma_start(
                        h1_h[m * 128:(m + 1) * 128, :], H1T[:, m % 4, :]
                    ).then_inc(s_sh1 if m % 2 == 0 else s_sh1b, 16)

                def xf_sub(m):
                    sync.wait_ge(s_vq, m + 1)
                    if m >= 2:
                        sync.wait_ge(s_m1, 4 * (m - 1))
                    sync.dma_start_transpose(
                        XF1[:, :, (m % 2) * 128:(m % 2 + 1) * 128], QX1[:, m % 2, :]
                    ).then_inc(s_tq1 if m % 2 == 0 else s_tq1b, 16)

                for n in range(NT * NS):
                    if n >= 4:
                        sync.wait_ge(s_su, n - 3)
                    sync.dma_start(
                        XBT[:, n % 4, :],
                        x_h[(n // 4) * TT + (n % 4) * 128:(n // 4) * TT + (n % 4 + 1) * 128, :],
                    ).then_inc(s_bxl if n % 2 == 0 else s_bxl2, 16)
                    if n >= 1:
                        xf_sub(n - 1)
                    if n >= 3:
                        st_sub(n - 3)
                xf_sub(NT * NS - 1)
                for m in range(NT * NS - 3, NT * NS):
                    st_sub(m)
                sync.wait_ge(s_sh1, 16 * (NT * NS // 2))
                sync.wait_ge(s_sh1b, 16 * (NT * NS // 2))
                sync.nop().then_inc(bar_b, 1)

            @blkB1.gpsimd
            def _(g):
                gw = Eng(g)
                g.wait_ge(bar_a, 5)
                g.dma_start(B1ROW[0:1, :], b1_h[0:1, :]).then_inc(s_gb1r, 16)
                for ch in range(32):
                    if ch >= 2:
                        g.wait_ge(s_vp, ch - 1)
                    g.dma_start(W1C[:, ch % 2, :], w1_h[ch * 128:(ch + 1) * 128, :]).then_inc(
                        s_gw0 if ch % 2 == 0 else s_gw1, 16)
                g.wait_ge(s_vp, 34)
                for ch in range(32):
                    if ch >= 2:
                        g.wait_ge(s_vp, 34 + ch - 1)
                    g.dma_start(W1C[:, ch % 2, :], w1_h[ch * 128:(ch + 1) * 128, :]).then_inc(
                        s_gx0 if ch % 2 == 0 else s_gx1, 16)
                    if ch >= 1:
                        g.wait_ge(s_vp, 34 + ch)
                        g.dma_start(w1q_h[(ch - 1) * 128:ch * 128, :], W1QB[:, (ch - 1) % 2, :]).then_inc(
                            s_gs0 if (ch - 1) % 2 == 0 else s_gs1, 16)
                g.wait_ge(s_vp, 34 + 32)
                g.dma_start(w1q_h[31 * 128:32 * 128, :], W1QB[:, 1, :]).then_inc(s_gs1, 16)
                g.nop().then_inc(bar_b, 1)

            @blkB1.vector
            def _(vec):
                v = Eng(vec)
                vec.wait_ge(bar_a, 5)
                for k, (dst, base) in enumerate([(G1REP, 0), (G1REP, 512), (BL1REP, 0), (BL1REP, 512)]):
                    vec.wait_ge(s_tb, 3 + k)
                    v.op(lambda dst=dst, base=base: nc.vector.tensor_copy(
                        dst[:, base:base + 512], PP1[0][:, 0:512]), ins=[PP1[0]], outs=[dst])
                    v.flush()
                    vec.engine_nop().then_inc(s_vbc, 1)
                for ch in range(32):
                    vec.wait_ge(s_gw0 if ch % 2 == 0 else s_gw1, gw_l(ch))
                    v.op(lambda ch=ch: nc.vector.reduce_sum(
                        WRED[:, ch:ch + 1], W1C[:, ch % 2, :], axis=AX, apply_absolute_value=True),
                        ins=[W1C], outs=[WRED])
                    v.flush()
                    vec.engine_nop().then_inc(s_vp, 1)
                v.op(lambda: nc.vector.reduce_sum(WRED[:, 32:33], WRED[:, 0:32], axis=AX),
                     ins=[WRED], outs=[WRED])
                v.flush()
                vec.engine_nop().then_inc(s_vp, 1)
                vec.wait_ge(s_mt, 1)
                v.op(lambda: nc.vector.tensor_copy(SCR1[0:1, 2:3], P1X[0:1, 0:1]), ins=[P1X], outs=[SCR1])
                v.op(lambda: nc.vector.tensor_scalar(SCR1[0:1, 2:3], SCR1[0:1, 2:3],
                                                     1.0 / (4096.0 * 1024.0), 1e-5, A.mult, A.max),
                     ins=[SCR1], outs=[SCR1])
                v.op(lambda: nc.vector.tensor_copy(SCR1[0:1, 1:2], SCR1[0:1, 2:3]), ins=[SCR1], outs=[SCR1])
                v.op(lambda: nc.vector.reciprocal(SCR1[0:1, 0:1], SCR1[0:1, 2:3]), ins=[SCR1], outs=[SCR1])
                v.flush()
                vec.engine_nop().then_inc(s_vp, 1)
                vec.wait_ge(s_tb, 7)
                v.op(lambda: nc.vector.tensor_copy(SW1[:, :], P1X[:, 0:2]), ins=[P1X], outs=[SW1])
                v.flush()
                vec.engine_nop().then_inc(s_vbc, 1)
                for ch in range(32):
                    vec.wait_ge(s_gx0 if ch % 2 == 0 else s_gx1, gw_l(ch))
                    if ch >= 2:
                        vec.wait_ge(s_gs0 if ch % 2 == 0 else s_gs1, gs_st(ch - 2))
                    v.op(lambda ch=ch: nc.vector.tensor_scalar(
                        R1[:, 0, :], W1C[:, ch % 2, :], SW1[:, 0:1], 0.5, A.mult, A.is_ge),
                        ins=[W1C, SW1], outs=[R1])
                    v.op(lambda ch=ch: nc.vector.tensor_scalar(
                        R1[:, 1, :], W1C[:, ch % 2, :], SW1[:, 0:1], -0.5, A.mult, A.is_lt),
                        ins=[W1C, SW1], outs=[R1])
                    v.op(lambda ch=ch: nc.vector.tensor_tensor(
                        W1QB[:, ch % 2, :], R1[:, 0, :], R1[:, 1, :], A.subtract),
                        ins=[R1], outs=[W1QB])
                    v.flush()
                    vec.engine_nop().then_inc(s_vp, 1)

                def v_pre(n):
                    sl = n % 4
                    vec.wait_ge(s_su, n + 1)
                    if n >= 4:
                        vec.wait_ge(s_mt, n - 2)
                    v.op(lambda n=n: nc.vector.tensor_tensor(
                        U1[:, n % 2, :], U1[:, n % 2, :], G1REP[:, :], A.mult), ins=[U1, G1REP], outs=[U1])
                    v.op(lambda n=n: nc.vector.tensor_tensor(
                        U1[:, n % 2, :], U1[:, n % 2, :], BL1REP[:, :], A.add), ins=[U1, BL1REP], outs=[U1])
                    v.op(lambda sl=sl, n=n: nc.vector.reduce_max(
                        TCOL[:, sl:sl + 1], U1[:, n % 2, :], axis=AX, apply_absolute_value=True),
                        ins=[U1], outs=[TCOL])
                    v.op(lambda sl=sl: nc.vector.tensor_scalar(
                        TC2[:, sl:sl + 1], TCOL[:, sl:sl + 1], 1e-5, 1.0 / 127.0, A.max, A.mult),
                        ins=[TCOL], outs=[TC2])
                    v.op(lambda sl=sl, n=n: nc.vector.reciprocal(
                        ST_S1[:, n:n + 1], TC2[:, sl:sl + 1]), ins=[TC2], outs=[ST_S1])
                    v.op(lambda sl=sl, n=n: nc.vector.tensor_scalar(
                        ST_D1[:, n:n + 1], TC2[:, sl:sl + 1], SW1[:, 1:2], None, A.mult),
                        ins=[TC2, SW1], outs=[ST_D1])
                    v.op(lambda sl=sl, n=n: nc.vector.tensor_scalar(
                        DQ1B[:, sl:sl + 1], ST_S1[:, n:n + 1], SW1[:, 0:1], None, A.mult),
                        ins=[ST_S1, SW1], outs=[DQ1B])
                    v.flush()
                    vec.engine_nop().then_inc(s_vd, 1)

                def v_dq(m):
                    sl = m % 4
                    vec.wait_ge(s_mt, m + 2)
                    if m >= 4:
                        vec.wait_ge(s_m1, 4 * (m - 3))
                    v.op(lambda sl=sl: nc.vector.tensor_copy(
                        DQ1R[0:1, sl * 128:(sl + 1) * 128], PB1DQ[0:1, sl * 128:(sl + 1) * 128]),
                        ins=[PB1DQ], outs=[DQ1R])
                    v.flush()
                    vec.engine_nop().then_inc(s_vr, 1)

                def v_peel(m):
                    if m >= 2:
                        vec.wait_ge(s_tq1 if m % 2 == 0 else s_tq1b,
                                    16 * (m // 2))
                    vec.wait_ge(s_ls, m + 1)
                    if ROUND_MODE == "fp16":
                        _round_q(v, nc, U1[:, m % 2, :], ST_S1[:, m:m + 1], QX1[:, m % 2, :],
                                 None, None, None, L16[:, :])
                    else:
                        Rh = R1[:, m % 2, :]
                        for k in range(8):
                            th = float(128 >> k)
                            v.op(lambda th=th, Rh=Rh: nc.vector.tensor_scalar(
                                B1L[:, :], Rh, th, -th, A.is_ge, A.mult), ins=[Rh], outs=[B1L])
                            v.op(lambda Rh=Rh: nc.vector.tensor_tensor(
                                Rh, Rh, B1L[:, :], A.add), ins=[Rh, B1L], outs=[Rh])
                        v.op(lambda Rh=Rh: nc.vector.tensor_scalar(
                            Rh, Rh, -0.5, None, A.add), ins=[Rh], outs=[Rh])
                        v.op(lambda m=m, Rh=Rh: nc.vector.scalar_tensor_tensor(
                            QX1[:, m % 2, :], U1[:, m % 2, :], ST_S1[:, m:m + 1], Rh,
                            A.mult, A.subtract), ins=[U1, ST_S1, R1], outs=[QX1])
                    v.flush()
                    vec.engine_nop().then_inc(s_vq, 1)

                for n in range(NT * NS):
                    if n >= 1:
                        v_dq(n - 1)
                        v_peel(n - 1)
                    v_pre(n)
                v_dq(NT * NS - 1)
                v_peel(NT * NS - 1)
                vec.wait_ge(s_sg, 4 * NT * NS)
                for col in range(32):
                    v.op(lambda col=col: nc.vector.reduce_sum(
                        ST_GS[:, col:col + 1], GT4[:, col, :], axis=AX),
                        ins=[GT4], outs=[ST_GS])
                v.flush()
                vec.engine_nop().then_inc(s_vgs, 1)
                vec.nop().then_inc(bar_b, 1)

            @blkB1.scalar
            def _(sc):
                s_ = Eng(sc)
                sc.wait_ge(bar_a, 5)

                def s_u1(n):
                    sl = n % 4
                    sc.wait_ge(s_bxl if n % 2 == 0 else s_bxl2, 16 * (n // 2 + 1))
                    if n >= 2:
                        sc.wait_ge(s_vq, n - 1)
                    s_.op(lambda sl=sl, n=n: nc.scalar.activation(
                        U1[:, n % 2, :], XBT[:, sl, :], ACT.Identity,
                        bias=ST_F[:, n:n + 1], scale=ST_E[:, n:n + 1]),
                        ins=[XBT, ST_E, ST_F], outs=[U1])
                    s_.flush()
                    sc.nop().then_inc(s_su, 1)

                def s_z0(m):
                    sc.wait_ge(s_vd, m + 1)
                    s_.op(lambda m=m: nc.scalar.activation(
                        R1[:, m % 2, :], U1[:, m % 2, :], ACT.Identity,
                        bias=LADB[:, 8:9], scale=ST_S1[:, m:m + 1]),
                        ins=[U1, ST_S1, LADB], outs=[R1])
                    s_.flush()
                    sc.nop().then_inc(s_ls, 1)

                def s_gel(m):
                    sl = m % 4
                    if m >= 2:
                        sc.wait_ge(s_sh1 if m % 2 == 0 else s_sh1b,
                                   16 * (m // 2))
                    for gix in range(4):
                        gl = 4 * m + gix
                        sc.wait_ge(s_m1, gl + 1)
                        s_.op(lambda sl=sl, gix=gix, gl=gl, m=m: nc.scalar.activation(
                            H1T[:, sl, 1024 * gix:1024 * (gix + 1)], PP1[gl % 2][:, :], ACT.Gelu,
                            scale=ST_D1[:, m:m + 1],
                            accum_out=GT4[:, m, gix:gix + 1]),
                            ins=[PP1[gl % 2], ST_D1], outs=[H1T, GT4])
                        s_.flush()
                        sc.nop().then_inc(s_sg, 1)
                    s_.op(lambda sl=sl, m=m: nc.scalar.activation(
                        SQD[:, :], H1T[:, sl, :], ACT.Square, accum_out=ST_SQ[:, m:m + 1]),
                        ins=[H1T], outs=[SQD, ST_SQ])
                    s_.flush()
                    sc.nop().then_inc(s_ss, 1)

                for n in range(NT * NS):
                    s_u1(n)
                    if n >= 1:
                        s_z0(n - 1)
                    if n >= 2:
                        s_gel(n - 2)
                s_z0(NT * NS - 1)
                s_gel(NT * NS - 2)
                s_gel(NT * NS - 1)
                sc.nop().then_inc(bar_b, 1)

            @blkB1.tensor
            def _(t):
                t.wait_ge(bar_a, 5)
                t.wait_ge(s_gb1r, 16)
                t.wait_ge(s_br1, 32)
                for k, (dst, base) in enumerate([(G1REP, 0), (G1REP, 512), (BL1REP, 0), (BL1REP, 512)]):
                    t.wait_ge(s_vbc, 2 + k)
                    t.matmul(PP1[0][:, 0:512], ONF[0:1, :], dst[0:1, base:base + 512],
                             start=True, stop=True).then_inc(s_tb, 1)
                t.wait_ge(s_vbc, 6)
                t.wait_ge(s_vp, 33)
                t.matmul(P1X[0:1, 0:1], WRED[:, 32:33], ONES1[:, :], start=True, stop=True).then_inc(s_mt, 1)
                t.wait_ge(s_vp, 34)
                t.matmul(P1X[:, 0:2], ONF[0:1, :], SCR1[0:1, 0:2], start=True, stop=True).then_inc(s_tb, 1)

                def m1_sub(m):
                    for gix in range(4):
                        gl = 4 * m + gix
                        if gix == 0:
                            if m == 0:
                                t.wait_ge(s_tw1w, 16 * 8)
                            t.wait_ge(s_tq1 if m % 2 == 0 else s_tq1b, 16 * (m // 2 + 1))
                            t.wait_ge(s_vr, m + 1)
                        if gl >= 2:
                            t.wait_ge(s_sg, gl - 1)
                        pp = PP1[gl % 2]
                        ins = None
                        for half in range(2):
                            hi = 1024 * gix + 512 * half
                            for dc in range(8):
                                t.matmul(pp[:, 512 * half:512 * (half + 1)],
                                         XF1[:, dc, (m % 2) * 128:(m % 2 + 1) * 128],
                                         W1T[:, dc, hi:hi + 512],
                                         start=(dc == 0), stop=False)
                            ins = t.matmul(pp[:, 512 * half:512 * (half + 1)],
                                           DQ1R[0:1, (m % 4) * 128:(m % 4 + 1) * 128],
                                           B1ROW[0:1, hi:hi + 512],
                                           start=False, stop=True)
                        ins.then_inc(s_m1, 1)

                for n in range(NT * NS):
                    t.wait_ge(s_vd, n + 1)
                    if n >= 4:
                        t.wait_ge(s_vr, n - 3)
                    t.transpose(PB1DQ[0:1, (n % 4) * 128:(n % 4 + 1) * 128],
                                DQ1B[:, n % 4:n % 4 + 1], IDENT[:, :]).then_inc(s_mt, 1)
                    if n >= 1:
                        m1_sub(n - 1)
                m1_sub(NT * NS - 1)
                t.nop().then_inc(bar_b, 1)

        with ExitStack() as stW2o:
            W2T = stW2o.enter_context(nc.sbuf_tensor("w2t", [128, 32, D], BF16))
            with ExitStack() as stW2:
                S = lambda *aa: stW2.enter_context(nc.sbuf_tensor(*aa))
                W2C = S("w2c", [128, 2, H], F32)
                W2QB = S("w2qb", [128, 2, H], BF16)
                WL1b = S("wl1b", [128, H], F32)
                WL2b = S("wl2b", [128, H], F32)
                P2X = stW2.enter_context(nc.psum_tensor("p2x", [128, 2], F32))
                blkW2 = stW2.enter_context(nc.Block())

                g2w_l = lambda ch: 16 * (16 + ch // 2 + 1)
                g2s_st = lambda ch: 16 * (16 + ch // 2 + 1)

                @blkW2.sync
                def _(sync):
                    sync.wait_ge(bar_b, 5)
                    sync.wait_ge(s_gs0, 16 * 20)
                    sync.wait_ge(s_gs1, 16 * 20)
                    for hc in range(32):
                        sync.dma_start_transpose(
                            W2T[:, hc, :], w2q_h[:, hc * 128:(hc + 1) * 128]).then_inc(s_tw2w, 16)
                    sync.nop().then_inc(bar_c, 1)

                @blkW2.gpsimd
                def _(g):
                    gw = Eng(g)
                    g.wait_ge(bar_b, 5)
                    for ch in range(8):
                        if ch >= 2:
                            g.wait_ge(s_vp2, ch - 1)
                        g.dma_start(W2C[:, ch % 2, :], w2_h[ch * 128:(ch + 1) * 128, :]).then_inc(
                            s_gw0 if ch % 2 == 0 else s_gw1, 16)
                    g.wait_ge(s_vp2, 10)
                    for ch in range(8):
                        if ch >= 2:
                            g.wait_ge(s_vp2, 10 + ch - 1)
                        g.dma_start(W2C[:, ch % 2, :], w2_h[ch * 128:(ch + 1) * 128, :]).then_inc(
                            s_gx0 if ch % 2 == 0 else s_gx1, 16)
                        if ch >= 1:
                            g.wait_ge(s_vp2, 10 + ch)
                            g.dma_start(w2q_h[(ch - 1) * 128:ch * 128, :], W2QB[:, (ch - 1) % 2, :]).then_inc(
                                s_gs0 if (ch - 1) % 2 == 0 else s_gs1, 16)
                    g.wait_ge(s_vp2, 10 + 8)
                    g.dma_start(w2q_h[7 * 128:8 * 128, :], W2QB[:, 1, :]).then_inc(s_gs1, 16)
                    g.nop().then_inc(bar_c, 1)

                @blkW2.vector
                def _(vec):
                    v = Eng(vec)
                    vec.wait_ge(bar_b, 5)
                    for ch in range(8):
                        vec.wait_ge(s_gw0 if ch % 2 == 0 else s_gw1, g2w_l(ch))
                        v.op(lambda ch=ch: nc.vector.reduce_sum(
                            WRED[:, ch:ch + 1], W2C[:, ch % 2, :], axis=AX, apply_absolute_value=True),
                            ins=[W2C], outs=[WRED])
                        v.flush()
                        vec.engine_nop().then_inc(s_vp2, 1)
                    v.op(lambda: nc.vector.reduce_sum(WRED[:, 32:33], WRED[:, 0:8], axis=AX),
                         ins=[WRED], outs=[WRED])
                    v.flush()
                    vec.engine_nop().then_inc(s_vp2, 1)
                    vec.wait_ge(s_mt, NT * NS + 2)
                    v.op(lambda: nc.vector.tensor_copy(SCR1[0:1, 2:3], P2X[0:1, 0:1]), ins=[P2X], outs=[SCR1])
                    v.op(lambda: nc.vector.tensor_scalar(SCR1[0:1, 2:3], SCR1[0:1, 2:3],
                                                         1.0 / (4096.0 * 1024.0), 1e-5, A.mult, A.max),
                         ins=[SCR1], outs=[SCR1])
                    v.op(lambda: nc.vector.tensor_copy(SCR1[0:1, 1:2], SCR1[0:1, 2:3]), ins=[SCR1], outs=[SCR1])
                    v.op(lambda: nc.vector.reciprocal(SCR1[0:1, 0:1], SCR1[0:1, 2:3]), ins=[SCR1], outs=[SCR1])
                    v.flush()
                    vec.engine_nop().then_inc(s_vp2, 1)
                    vec.wait_ge(s_tb, 8)
                    v.op(lambda: nc.vector.tensor_copy(SW2[:, :], P2X[:, 0:2]), ins=[P2X], outs=[SW2])
                    v.flush()
                    vec.engine_nop().then_inc(s_vbc, 1)
                    for ch in range(8):
                        vec.wait_ge(s_gx0 if ch % 2 == 0 else s_gx1, g2w_l(ch))
                        if ch >= 2:
                            vec.wait_ge(s_gs0 if ch % 2 == 0 else s_gs1, g2s_st(ch - 2))
                        v.op(lambda ch=ch: nc.vector.tensor_scalar(
                            WL1b[:, :], W2C[:, ch % 2, :], SW2[:, 0:1], 0.5, A.mult, A.is_ge),
                            ins=[W2C, SW2], outs=[WL1b])
                        v.op(lambda ch=ch: nc.vector.tensor_scalar(
                            WL2b[:, :], W2C[:, ch % 2, :], SW2[:, 0:1], -0.5, A.mult, A.is_lt),
                            ins=[W2C, SW2], outs=[WL2b])
                        v.op(lambda ch=ch: nc.vector.tensor_tensor(
                            W2QB[:, ch % 2, :], WL1b[:, :], WL2b[:, :], A.subtract),
                            ins=[WL1b, WL2b], outs=[W2QB])
                        v.flush()
                        vec.engine_nop().then_inc(s_vp2, 1)
                    vec.nop().then_inc(bar_c, 1)

                @blkW2.scalar
                def _(sc):
                    sc.wait_ge(bar_b, 5)
                    sc.nop().then_inc(bar_c, 1)

                @blkW2.tensor
                def _(t):
                    t.wait_ge(bar_b, 5)
                    t.wait_ge(s_vp2, 9)
                    t.matmul(P2X[0:1, 0:1], WRED[:, 32:33], ONES1[:, :], start=True, stop=True).then_inc(s_mt, 1)
                    t.wait_ge(s_vp2, 10)
                    t.matmul(P2X[:, 0:2], ONF[0:1, :], SCR1[0:1, 0:2], start=True, stop=True).then_inc(s_tb, 1)
                    t.nop().then_inc(bar_c, 1)

            with ExitStack() as stB2:
                S = lambda *aa: stB2.enter_context(nc.sbuf_tensor(*aa))
                G2REP = S("g2rep", [128, H], BF16)
                BL2REP = S("bl2rep", [128, H], BF16)
                B2REP = S("b2rep", [128, D], BF16)
                H1B = S("h1b", [128, 2, H], BF16)
                U2 = S("u2", [128, 2, H], F32)
                QX2 = S("qx2", [128, H], BF16)
                QX2F = S("qx2f", [128, 32, 128], BF16)
                OUTF = S("outf", [128, 2, D], F32)
                RL = S("rl", [128, 2, H], F32)
                SGN = S("sgn", [128, 2, 2, 2048], BF16)
                L16b = S("l16b", [128, H], FP16) if ROUND_MODE == "fp16" else None
                PP2 = [stB2.enter_context(nc.psum_tensor(f"pp2{i}", [128, 1024], F32)) for i in range(2)]
                blkB2 = stB2.enter_context(nc.Block())

                ga_sem = lambda m: s_ga if m % 2 == 0 else s_ga2

                bh_sem = lambda n: s_bh if n % 2 == 0 else s_bh2
                bh_h1 = lambda n: 16 * (n // 2 + 1)

                @blkB2.sync
                def _(sync):
                    def tq_sub(m):
                        sync.wait_ge(s_vq2, m + 1)
                        if m >= 1:
                            sync.wait_ge(s_m2, m)
                        sync.dma_start_transpose(QX2F[:, :, :], QX2[:, :]).then_inc(s_tq2, 16)

                    sync.wait_ge(bar_c, 5)
                    for n in range(NT * NS):
                        if n >= 1:
                            tq_sub(n - 1)
                    tq_sub(NT * NS - 1)
                    sync.wait_ge(s_ga, 16 * (NT * NS // 2))
                    sync.wait_ge(s_ga2, 16 * (NT * NS // 2))
                    sync.nop()

                @blkB2.gpsimd
                def _(g):
                    gw = Eng(g)
                    g.wait_ge(bar_c, 5)
                    g.dma_start(G2REP[0:1, :], g2_h[0:1, :]).then_inc(s_gr, 16)
                    g.dma_start(BL2REP[0:1, :], bl2_h[0:1, :]).then_inc(s_gr, 16)
                    g.dma_start(B2REP[0:1, :], b2_h[0:1, :]).then_inc(s_gr, 16)
                    def h1b_load(n):
                        if n >= 2:
                            g.wait_ge(s_su2, n - 1)
                        g.dma_start(
                            H1B[:, n % 2, :],
                            h1_h[(n // 4) * TT + (n % 4) * 128:(n // 4) * TT + (n % 4 + 1) * 128, :]
                        ).then_inc(bh_sem(n), 16)

                    for n in (0, 1, 2):
                        h1b_load(n)
                    for m in range(NT * NS - 2):
                        if m + 3 < NT * NS:
                            h1b_load(m + 3)
                        g.wait_ge(s_ve2, m + 1)
                        g.dma_start(
                            out_h[(m // 4) * TT + (m % 4) * 128:(m // 4) * TT + (m % 4 + 1) * 128, :],
                            OUTF[:, m % 2, :], accum_op=A.add,
                        ).then_inc(ga_sem(m), 16)
                    for m in (NT * NS - 2, NT * NS - 1):
                        g.wait_ge(s_ve2, m + 1)
                        g.dma_start(
                            out_h[(m // 4) * TT + (m % 4) * 128:(m // 4) * TT + (m % 4 + 1) * 128, :],
                            OUTF[:, m % 2, :], accum_op=A.add,
                        ).then_inc(ga_sem(m), 16)
                    g.wait_ge(s_ga, 16 * (NT * NS // 2))
                    g.wait_ge(s_ga2, 16 * (NT * NS // 2))

                @blkB2.vector
                def _(vec):
                    v = Eng(vec)
                    vec.wait_ge(bar_c, 5)
                    vec.wait_ge(s_vgs, 1)
                    bt = [("g2", c) for c in range(8)] + [("bl2", c) for c in range(8)] + [("b2", c) for c in range(2)]
                    for k, (which, c) in enumerate(bt):
                        vec.wait_ge(s_tb, 9 + k)
                        if which == "g2":
                            v.op(lambda c=c: nc.vector.tensor_copy(
                                G2REP[:, 512 * c:512 * (c + 1)], PP2[0][:, 0:512]), ins=[PP2[0]], outs=[G2REP])
                        elif which == "bl2":
                            v.op(lambda c=c: nc.vector.tensor_copy(
                                BL2REP[:, 512 * c:512 * (c + 1)], PP2[0][:, 0:512]), ins=[PP2[0]], outs=[BL2REP])
                        else:
                            v.op(lambda c=c: nc.vector.tensor_copy(
                                B2REP[:, 512 * c:512 * (c + 1)], PP2[0][:, 0:512]), ins=[PP2[0]], outs=[B2REP])
                        v.flush()
                        vec.engine_nop().then_inc(s_vbc, 1)
                    def stats_tile(i):
                        c0 = 4 * i
                        h8 = 8 * (i % 2)
                        if i >= 2:
                            vec.wait_ge(s_su2, 4 * (i - 1))
                        v.op(lambda c0=c0: nc.vector.tensor_scalar(
                            TCOL[:, 0:4], ST_GS[:, c0:c0 + 4], 1.0 / 4096.0, None, A.mult),
                            ins=[ST_GS], outs=[TCOL])
                        v.op(lambda c0=c0: nc.vector.tensor_scalar(
                            TCOL[:, 4:8], ST_SQ[:, c0:c0 + 4], 1.0 / 4096.0, None, A.mult),
                            ins=[ST_SQ], outs=[TCOL])
                        v.op(lambda: nc.vector.tensor_tensor(TC2[:, 0:4], TCOL[:, 0:4], TCOL[:, 0:4], A.mult),
                             ins=[TCOL], outs=[TC2])
                        v.op(lambda: nc.vector.tensor_tensor(TC2[:, 4:8], TCOL[:, 4:8], TC2[:, 0:4], A.subtract),
                             ins=[TCOL, TC2], outs=[TC2])
                        v.op(lambda: nc.vector.tensor_scalar(WPK[:, 0:4], TC2[:, 4:8], 1e-5, None, A.add),
                             ins=[TC2], outs=[WPK])
                        _heron(v, nc, WPK, WPH, WPY, 4)
                        v.op(lambda h8=h8: nc.vector.reciprocal(TC3[:, h8:h8 + 4], WPY[:, 0:4]),
                             ins=[WPY], outs=[TC3])
                        v.op(lambda h8=h8: nc.vector.scalar_tensor_tensor(
                            TC3[:, h8 + 4:h8 + 8], TCOL[:, 0:4], -1.0, TC3[:, h8:h8 + 4], A.mult, A.mult),
                            ins=[TCOL, TC3], outs=[TC3])
                        v.flush()
                        vec.engine_nop().then_inc(s_vst2, 1)

                    def gam_sub(n):
                        vec.wait_ge(s_su2, n + 1)
                        v.op(lambda n=n: nc.vector.tensor_tensor(
                            U2[:, n % 2, :], U2[:, n % 2, :], G2REP[:, :], A.mult),
                            ins=[U2, G2REP], outs=[U2])
                        v.flush()

                    def bet_sub(n):
                        v.op(lambda n=n: nc.vector.tensor_tensor(
                            U2[:, n % 2, :], U2[:, n % 2, :], BL2REP[:, :], A.add),
                            ins=[U2, BL2REP], outs=[U2])
                        v.flush()

                    def scales_sub(n):
                        sl = n % 4
                        v.op(lambda sl=sl, n=n: nc.vector.reduce_max(
                            TCOL[:, sl:sl + 1], U2[:, n % 2, :], axis=AX, apply_absolute_value=True),
                            ins=[U2], outs=[TCOL])
                        v.op(lambda sl=sl: nc.vector.tensor_scalar(
                            TC2[:, sl:sl + 1], TCOL[:, sl:sl + 1], 1e-5, 1.0 / 127.0, A.max, A.mult),
                            ins=[TCOL], outs=[TC2])
                        v.op(lambda sl=sl, n=n: nc.vector.reciprocal(
                            ST_S2[:, n:n + 1], TC2[:, sl:sl + 1]), ins=[TC2], outs=[ST_S2])
                        v.op(lambda sl=sl, n=n: nc.vector.tensor_scalar(
                            ST_D2[:, n:n + 1], TC2[:, sl:sl + 1], SW2[:, 1:2], None, A.mult),
                            ins=[TC2, SW2], outs=[ST_D2])
                        v.flush()
                        vec.engine_nop().then_inc(s_vsc, 1)

                    def peel_sub(m, inj):
                        base = 16 * m
                        for b in range(8):
                            cb = _LAD_C[b]
                            for u in range(2):
                                vec.wait_ge(s_pg, base + 2 * b + u + 1)
                                v.op(lambda m=m, u=u, cb=cb: nc.vector.scalar_tensor_tensor(
                                    RL[:, m % 2, 2048 * u:2048 * (u + 1)], SGN[:, m % 2, u, :], -cb,
                                    RL[:, m % 2, 2048 * u:2048 * (u + 1)], A.mult, A.add),
                                    ins=[SGN, RL], outs=[RL])
                                v.flush()
                                vec.engine_nop().then_inc(s_lv, 1)
                            if b in inj:
                                inj[b]()

                    def finals_sub(m):
                        if m >= 1:
                            vec.wait_ge(s_tq2, 16 * m)
                        for u in range(2):
                            Rh = RL[:, m % 2, 2048 * u:2048 * (u + 1)]
                            v.op(lambda m=m, u=u, Rh=Rh: nc.vector.scalar_tensor_tensor(
                                QX2[:, 2048 * u:2048 * (u + 1)], U2[:, m % 2, 2048 * u:2048 * (u + 1)],
                                ST_S2[:, m:m + 1], Rh, A.mult, A.subtract),
                                ins=[U2, ST_S2, RL], outs=[QX2])
                        v.flush()
                        vec.engine_nop().then_inc(s_vq2, 1)

                    def outf_sub(m):
                        if m >= 2:
                            vec.wait_ge(ga_sem(m), 16 * (m // 2))
                        vec.wait_ge(s_m2, m + 1)
                        v.op(lambda m=m: nc.vector.scalar_tensor_tensor(
                            OUTF[:, m % 2, :], PP2[m % 2][:, :], ST_D2[:, m:m + 1],
                            B2REP[:, :], A.mult, A.add),
                            ins=[PP2[m % 2], ST_D2, B2REP], outs=[OUTF])
                        v.flush()
                        vec.engine_nop().then_inc(s_ve2, 1)

                    for n in range(NT * NS):
                        if n % 4 == 2 and n + 2 < NT * NS:
                            stats_tile((n + 2) // 4)
                        if n == 0:
                            stats_tile(0)
                            vec.wait_ge(s_vbc, 24)
                            gam_sub(0)
                            bet_sub(0)
                            scales_sub(0)
                        else:
                            inj = {2: (lambda n=n: gam_sub(n)),
                                   3: (lambda n=n: bet_sub(n)),
                                   4: (lambda n=n: scales_sub(n))}
                            if n >= 2:
                                inj[5] = (lambda n=n: outf_sub(n - 2))
                            peel_sub(n - 1, inj)
                            finals_sub(n - 1)
                    mL = NT * NS - 1
                    peel_sub(mL, {5: (lambda: outf_sub(mL - 1))})
                    finals_sub(mL)
                    outf_sub(mL)

                @blkB2.scalar
                def _(sc):
                    s_ = Eng(sc)
                    sc.wait_ge(bar_c, 5)

                    def emit_sign(m, b, u):
                        s_.op(lambda m=m, b=b, u=u: nc.scalar.activation(
                            SGN[:, m % 2, u, :], RL[:, m % 2, 2048 * u:2048 * (u + 1)], ACT.Sign,
                            bias=LADB[:, b:b + 1]),
                            ins=[RL, LADB], outs=[SGN])
                        s_.flush()
                        sc.nop().then_inc(s_pg, 1)

                    def z0_sub(m):
                        sc.wait_ge(s_vsc, m + 1)
                        s_.op(lambda m=m: nc.scalar.activation(
                            RL[:, m % 2, :], U2[:, m % 2, :], ACT.Identity,
                            bias=LADB[:, 9:10], scale=ST_S2[:, m:m + 1]),
                            ins=[U2, ST_S2, LADB], outs=[RL])
                        s_.flush()

                    def tail_signs(m):
                        for b in range(2, 8):
                            for u in range(2):
                                sc.wait_ge(s_lv, 16 * m + 2 * (b - 1) + u + 1)
                                emit_sign(m, b, u)
                            if b == 5 and m + 1 < NT * NS:
                                z0_sub(m + 1)
                        if m + 1 < NT * NS:
                            for u in range(2):
                                if m >= 1:
                                    sc.wait_ge(s_lv, 16 * (m - 1) + 15 + u)
                                emit_sign(m + 1, 0, u)

                    for n in range(NT * NS):
                        i, sl = divmod(n, 4)
                        m = n - 1
                        if n >= 1:
                            for u in range(2):
                                sc.wait_ge(s_lv, 16 * m + u + 1)
                                emit_sign(m, 1, u)
                        if sl == 0:
                            sc.wait_ge(s_vst2, i + 1)
                        sc.wait_ge(bh_sem(n), bh_h1(n))
                        if n >= 2:
                            sc.wait_ge(s_vq2, n - 1)
                        h8 = 8 * (i % 2)
                        s_.op(lambda sl=sl, n=n, h8=h8: nc.scalar.activation(
                            U2[:, n % 2, :], H1B[:, n % 2, :], ACT.Identity,
                            bias=TC3[:, h8 + 4 + sl:h8 + 5 + sl], scale=TC3[:, h8 + sl:h8 + sl + 1]),
                            ins=[H1B, TC3], outs=[U2])
                        s_.flush()
                        sc.nop().then_inc(s_su2, 1)
                        if n == 0:
                            z0_sub(0)
                            for u in range(2):
                                emit_sign(0, 0, u)
                        else:
                            tail_signs(m)
                    m = NT * NS - 1
                    for u in range(2):
                        sc.wait_ge(s_lv, 16 * m + u + 1)
                        emit_sign(m, 1, u)
                    tail_signs(m)

                @blkB2.tensor
                def _(t):
                    t.wait_ge(bar_c, 5)
                    t.wait_ge(s_gr, 48)
                    bt = [("g2", c) for c in range(8)] + [("bl2", c) for c in range(8)] + [("b2", c) for c in range(2)]
                    for k, (which, c) in enumerate(bt):
                        t.wait_ge(s_vbc, 8 + k)
                        if which == "g2":
                            t.matmul(PP2[0][:, 0:512], ONB[0:1, :], G2REP[0:1, 512 * c:512 * (c + 1)],
                                     start=True, stop=True).then_inc(s_tb, 1)
                        elif which == "bl2":
                            t.matmul(PP2[0][:, 0:512], ONB[0:1, :], BL2REP[0:1, 512 * c:512 * (c + 1)],
                                     start=True, stop=True).then_inc(s_tb, 1)
                        else:
                            t.matmul(PP2[0][:, 0:512], ONB[0:1, :], B2REP[0:1, 512 * c:512 * (c + 1)],
                                     start=True, stop=True).then_inc(s_tb, 1)
                    t.wait_ge(s_vbc, 8 + 18)
                    t.wait_ge(s_tw2w, 16 * 32)
                    for i in range(NT):
                        for s in range(NS):
                            n = 4 * i + s
                            t.wait_ge(s_tq2, 16 * (n + 1))
                            if n >= 2:
                                t.wait_ge(s_ve2, n - 1)
                            pp = PP2[n % 2]
                            ins = None
                            for half in range(2):
                                for hc in range(32):
                                    ins = t.matmul(pp[:, 512 * half:512 * (half + 1)],
                                                   QX2F[:, hc, :],
                                                   W2T[:, hc, 512 * half:512 * (half + 1)],
                                                   start=(hc == 0), stop=(hc == 31))
                            ins.then_inc(s_m2, 1)

    return nc


_CACHE = {}


def kernel(L0_attn_x, L0_attn_gamma, L0_attn_w_qkva, L0_ffn_ln1_g, L0_ffn_ln1_b,
           L0_ffn_w1, L0_ffn_b1, L0_ffn_ln2_g, L0_ffn_ln2_b, L0_ffn_w2, L0_ffn_b2):
    if "nc" not in _CACHE:
        _CACHE["nc"] = build_program()
    nc = _CACHE["nc"]
    x = np.ascontiguousarray(np.asarray(L0_attn_x, dtype=np.float32))
    row = lambda a: np.ascontiguousarray(np.asarray(a, dtype=np.float32).reshape(1, -1))
    mat = lambda a: np.ascontiguousarray(np.asarray(a, dtype=np.float32))
    base = {
        "wqkva": mat(L0_attn_w_qkva),
        "gamma": row(L0_attn_gamma),
        "ln1g": row(L0_ffn_ln1_g),
        "ln1b": row(L0_ffn_ln1_b),
        "w1": mat(L0_ffn_w1),
        "b1": row(L0_ffn_b1),
        "ln2g": row(L0_ffn_ln2_g),
        "ln2b": row(L0_ffn_ln2_b),
        "w2": mat(L0_ffn_w2),
        "b2": row(L0_ffn_b2),
    }
    in_maps = [dict(base, x=np.ascontiguousarray(x[c])) for c in range(N_CORES)]
    res = run_bass_kernel_spmd(nc, in_maps, core_ids=list(range(N_CORES)))
    out = np.stack([np.asarray(res.results[c]["out"]) for c in range(N_CORES)], axis=0)
    return out.astype(np.float32)
